# version 01
# baseline (speedup 1.0000x reference)
import numpy as np
import ml_dtypes
from contextlib import ExitStack
import concourse.bass as bass
import concourse.mybir as mybir
from concourse.bass_utils import run_bass_kernel_spmd

F32 = mybir.dt.float32
BF16 = mybir.dt.bfloat16
U32 = mybir.dt.uint32
AF = mybir.ActivationFunctionType
ALU = mybir.AluOpType
AX = mybir.AxisListType

NCORES = 8
T = 2048
HB = 258
TTW = HB + T
KT = 2304
NKT = 18
D = 1024
NB = 8
H = 16
HD = 64
R = 256
DI = 2816
NFB1 = 22
QKS = 8.0
DEPTH = 6
BF = ml_dtypes.bfloat16

CONSTVALS = [8.0, 0.5, -0.5, -1.0, 2.0, 1e-17, 1.0 / D, 1e-5, 1e-6, -8.0, 1.0,
             1.0 / 1024.0]


def _f32(x):
    return np.ascontiguousarray(x, dtype=np.float32)


def host_prep(inputs, nlayers=DEPTH):
    x = _f32(inputs["L0_attn_x"][0])
    S = x.shape[0]
    qs_all = _f32(inputs["L0_ffn_q_scale"])
    ks_all = _f32(inputs["L0_ffn_k_scale"])
    Wqkv = _f32(inputs["L0_ffn_Wqkv"])
    Wo = _f32(inputs["L0_ffn_Wo"])
    lg = _f32(inputs["L0_ffn_ln_g"])
    lb = _f32(inputs["L0_ffn_ln_b"])
    W1 = _f32(inputs["L0_ffn_W1"])
    b1 = _f32(inputs["L0_ffn_b1"])
    W2 = _f32(inputs["L0_ffn_W2"])
    b2 = _f32(inputs["L0_ffn_b2"])
    L = nlayers

    sh = {}
    sh["wqT"] = _f32(inputs["L0_attn_wq"].T)
    sh["wkT"] = _f32(inputs["L0_attn_wk"].T)
    sh["wvT"] = _f32(inputs["L0_attn_wv"].T)
    sh["woT"] = _f32(inputs["L0_attn_wo"].T)
    rf = _f32(inputs["L0_attn_random_features"])
    rfd = np.zeros((128, H * R), np.float32)
    rfa = np.zeros((65, H * R), np.float32)
    for h in range(H):
        rfd[0:64, h * R:(h + 1) * R] = rf[h]
        rfd[64:128, h * R:(h + 1) * R] = rf[h]
        rfa[0:64, h * R:(h + 1) * R] = rf[h]
    rfa[64, :] = -1.0
    sh["rfd"], sh["rfa"] = rfd, rfa
    for nm, key in (("bq", "L0_attn_bq"), ("bk", "L0_attn_bk"), ("bo", "L0_attn_bo")):
        sh[nm] = np.ascontiguousarray(_f32(inputs[key]).reshape(NB, 128).T)
    sh["bvrow"] = _f32(inputs["L0_attn_bv"]).reshape(1, D)

    sh["wqkv_s"] = np.ascontiguousarray(
        Wqkv[:L].reshape(L, NB, 128, 24, 128).transpose(0, 3, 1, 2, 4))
    sh["wo_s"] = np.ascontiguousarray(
        Wo[:L].reshape(L, NB, 128, NB, 128).transpose(0, 3, 1, 2, 4))
    W1p = np.zeros((L, D, 2 * DI), np.float32)
    W1p[:, :, 0:2730] = W1[:L, :, 0:2730]
    W1p[:, :, DI:DI + 2730] = W1[:L, :, 2730:5460]
    w1v = W1p.reshape(L, NB, 128, 2, NFB1, 128)
    sh["w1_s"] = np.ascontiguousarray(
        w1v.transpose(0, 4, 1, 2, 3, 5).reshape(L, NFB1, NB, 128, 256))
    W2p = np.zeros((L, DI, D), np.float32)
    W2p[:, 0:2730, :] = W2[:L]
    sh["w2_s"] = np.ascontiguousarray(
        W2p.reshape(L, NFB1, 128, NB, 128).transpose(0, 3, 1, 2, 4))
    b1p = np.zeros((L, 2 * DI), np.float32)
    b1p[:, 0:2730] = b1[:L, 0:2730]
    b1p[:, DI:DI + 2730] = b1[:L, 2730:5460]
    sh["b1c"] = np.ascontiguousarray(b1p.reshape(L, 2 * NFB1, 128).transpose(0, 2, 1))
    sh["b2c"] = np.ascontiguousarray(b2[:L].reshape(L, NB, 128).transpose(0, 2, 1))
    glb = np.zeros((L, NB, 2, 128), np.float32)
    for l in range(L):
        glb[l, :, 0, :] = lg[l].reshape(NB, 128)
        glb[l, :, 1, :] = lb[l].reshape(NB, 128)
    sh["glb"] = glb
    ind = np.zeros((L, 2, NB, H, 128), np.float32)
    for l in range(L):
        for b in range(NB):
            ind[l, 0, b, 2 * b, 0:64] = qs_all[l]
            ind[l, 0, b, 2 * b + 1, 64:128] = qs_all[l]
            ind[l, 1, b, 2 * b, 0:64] = ks_all[l]
            ind[l, 1, b, 2 * b + 1, 64:128] = ks_all[l]
    sh["indqk"] = ind
    ind16 = np.zeros((NB, 128, H), np.float32)
    for b in range(NB):
        ind16[b, 0:64, 2 * b] = 1.0
        ind16[b, 64:128, 2 * b + 1] = 1.0
    sh["ind16"] = np.ascontiguousarray(ind16.transpose(1, 0, 2).reshape(128, NB * H))
    indh = np.zeros((128, 4), np.float32)
    indh[0:64, 0] = -0.5
    indh[64:128, 1] = -0.5
    indh[0:64, 2] = 0.5
    indh[64:128, 3] = 0.5
    sh["indh"] = indh
    srow = np.zeros((1, max(L, 1) * 128), np.float32)
    for l in range(L):
        srow[0, l * 128:l * 128 + 64] = qs_all[l]
        srow[0, l * 128 + 64:l * 128 + 128] = ks_all[l]
    sh["srow"] = srow
    sh["consts"] = np.tile(np.array(CONSTVALS, np.float32), (128, 1))
    sh["eye"] = np.eye(128, dtype=np.float32)

    maps = []
    for ci in range(NCORES):
        m = dict(sh)
        st = ci * T
        lo = st - HB
        xs = np.zeros((TTW, D), np.float32)
        a = max(lo, 0)
        xs[a - lo:TTW] = x[a:st + T]
        m["xT"] = np.ascontiguousarray(xs.T)
        tok = np.arange(st - 256, st + T, dtype=np.float64)
        half = np.arange(0, HD, 2, dtype=np.float64)
        inv_freq = 1.0 / 10000.0 ** (half / HD)
        fr = tok[None, :] * inv_freq[:, None]
        fr = np.concatenate([fr, fr], 0)
        base = (half + 0.4 * HD) / (1.4 * HD)
        power = (tok - S // 2) / 128.0
        sc = base[:, None] ** power[None, :]
        sc = np.concatenate([sc, sc], 0)
        cos, sin = np.cos(fr), np.sin(fr)
        sgn = np.ones((HD, 1))
        sgn[0:32] = -1.0
        aq = (cos * sc).astype(np.float32)
        bq = (sgn * sin * sc).astype(np.float32)
        ak = (cos / sc).astype(np.float32)
        bk_ = (sgn * sin / sc).astype(np.float32)
        m["rotAq"] = np.concatenate([aq[:, 256:], aq[:, 256:]], 0)
        m["rotBq"] = np.concatenate([bq[:, 256:], bq[:, 256:]], 0)
        m["rotAk"] = np.concatenate([ak, ak], 0)
        m["rotBk"] = np.concatenate([bk_, bk_], 0)
        kr = np.arange(128)[:, None]
        mm = np.arange(384)[None, :]
        interior = ((kr <= mm) & (mm <= kr + 256)).astype(np.float32)
        mk = np.zeros((128, 768), np.float32)
        mk[:, 0:384] = interior
        if ci > 0:
            mk[:, 384:512] = interior[:, 256:384]
            mk[:, 512:768] = interior[:, 128:384]
        m["masks"] = mk
        m["haloscale"] = np.full((128, 1), 0.0 if ci == 0 else 1.0, np.float32)
        left = (ci - 1) % NCORES
        m["hidx"] = (np.arange(128, dtype=np.uint32) + 128 * left).reshape(128, 1)
        maps.append(m)
    return maps


def build(nlayers=DEPTH, do_perf=True, dumps=()):
    nc = bass.Bass("TRN2", target_bir_lowering=False)
    es = ExitStack()
    L = nlayers

    def din(name, shape, dt=F32):
        return nc.dram_tensor(name, list(shape), dt, kind="ExternalInput").ap()

    def dout(name, shape, dt=F32):
        return nc.dram_tensor(name, list(shape), dt, kind="ExternalOutput").ap()

    def sbuf(name, shape, dt=BF16):
        return es.enter_context(nc.sbuf_tensor(name, list(shape), dt))

    def psm(name, shape, dt=F32):
        return es.enter_context(nc.psum_tensor(name, list(shape), dt))

    class Ctr:
        __slots__ = ("h", "n")

        def __init__(self, name):
            self.h = es.enter_context(nc.semaphore(name))
            self.n = 0

    d_xT = din("xT", [D, TTW])
    d_rotAq = din("rotAq", [128, T]); d_rotBq = din("rotBq", [128, T])
    d_rotAk = din("rotAk", [128, KT]); d_rotBk = din("rotBk", [128, KT])
    d_masks = din("masks", [128, 768])
    d_hs = din("haloscale", [128, 1])
    d_hidx = din("hidx", [128, 1], U32)
    d_consts = din("consts", [128, len(CONSTVALS)])
    d_wqT = din("wqT", [D, D]); d_wkT = din("wkT", [D, D])
    d_wvT = din("wvT", [D, D]); d_woT = din("woT", [D, D])
    d_rfd = din("rfd", [128, H * R]); d_rfa = din("rfa", [65, H * R])
    d_bq = din("bq", [128, NB]); d_bk = din("bk", [128, NB]); d_bo = din("bo", [128, NB])
    d_bvrow = din("bvrow", [1, D])
    d_eye = din("eye", [128, 128])
    d_wqkv = din("wqkv_s", [L, 24, NB, 128, 128])
    d_wo = din("wo_s", [L, NB, NB, 128, 128])
    d_w1 = din("w1_s", [L, NFB1, NB, 128, 256])
    d_w2 = din("w2_s", [L, NB, NFB1, 128, 128])
    d_b1c = din("b1c", [L, 128, 2 * NFB1])
    d_b2c = din("b2c", [L, 128, NB])
    d_glb = din("glb", [L, NB, 2, 128])
    d_ind = din("indqk", [L, 2, NB, H, 128])
    d_ind16 = din("ind16", [128, NB * H])
    d_indh = din("indh", [128, 4])
    d_srow = din("srow", [1, max(L, 1) * 128])
    d_out = dout("out", [D, T])
    dump_outs = {nm: dout("dump_" + nm, [D, T]) for nm in dumps}
    d_ssp = nc.dram_tensor("s_sp", [D, T], F32).ap()
    d_hbin = nc.dram_tensor("hb_in", [128, NB * HB], BF16).ap()
    d_hbout = nc.dram_tensor("hb_out", [NCORES * 128, NB * HB], BF16).ap()
    d_tbin = nc.dram_tensor("tb_in", [128, NB], BF16).ap()
    d_tbout = nc.dram_tensor("tb_out", [NCORES * 128, NB], BF16).ap()
    d_kvin = nc.dram_tensor("kv_in", [128, H * 130], F32).ap()
    d_kvout = nc.dram_tensor("kv_out", [128, H * 130], F32).ap()

    y = sbuf("y", [128, NB * TTW])

    def yb(b):
        return y[:, b * TTW:(b + 1) * TTW]

    v_tok = sbuf("v_tok", [128, NKT * H * 65])

    def vt(tt):
        return v_tok[:, tt * H * 65:(tt + 1) * H * 65]

    scr1 = sbuf("scr1", [128, NB * T])

    def s1b(b):
        return scr1[:, b * T:(b + 1) * T]

    pool = sbuf("pool", [128, NFB1 * 512])
    scr2 = pool
    qhat = sbuf("qhat", [128, T])
    khat = sbuf("khat", [128, KT])
    qfeat = pool[:, 0:KT]
    qsq = pool[:, KT:2 * KT]
    rott0 = pool[:, 2 * KT:3 * KT]
    rott1 = pool[:, 3 * KT:4 * KT]
    tabAq = sbuf("tabAq", [128, T]); tabBq = sbuf("tabBq", [128, T])
    tabAk = sbuf("tabAk", [128, KT]); tabBk = sbuf("tabBk", [128, KT])
    masks_s = sbuf("masks_s", [128, 768])
    Etile = None
    Eraw = None
    rnorm = sbuf("rnorm", [16, 512])
    rowsA = sbuf("rowsA", [1, TTW])
    rowsB = sbuf("rowsB", [2, TTW])
    rrow = sbuf("rrow", [1, 512])
    consts = sbuf("consts_s", [128, len(CONSTVALS)], F32)
    cbias = sbuf("cbias", [128, 2], F32)
    hsc = sbuf("hsc", [128, 1], F32)
    hidx_s = sbuf("hidx_s", [128, 1], U32)
    glbt = sbuf("glbt", [2, NB * 128])
    indq = sbuf("indq", [16, NB * 128])
    indk = sbuf("indk", [16, NB * 128])
    ind16s = sbuf("ind16s", [128, NB * H])
    eyef = sbuf("eyef", [128, 128], F32)
    trow = sbuf("trow", [16, 128])
    indhs = sbuf("indhs", [128, 4])
    b1s = sbuf("b1s", [128, 2 * NFB1], F32)
    b2s = sbuf("b2s", [128, NB], F32)
    bqs = sbuf("bqs", [128, NB], F32)
    bks = sbuf("bks", [128, NB], F32)
    bos = sbuf("bos", [128, NB], F32)
    srow_s = sbuf("srow_s", [1, 128])
    rfds = sbuf("rfds", [128, H * R])
    rfas = None
    onesr = sbuf("onesr", [1, 128])
    ones128 = sbuf("ones128", [128, 2])
    negones = sbuf("negones", [1, 128])
    n2tok = sbuf("n2tok", [128, NB * 32], F32)
    tcol = sbuf("tcol", [128, 2 * NB])
    wrow = qhat[0:1, 0:T]
    bvr = rowsB[0:1, 0:D]
    Etile = ([rott1[:, 768 + i * 384:768 + (i + 1) * 384] for i in range(4)]
             + [rott0[:, 512 + i * 384:512 + (i + 1) * 384] for i in range(2)])
    Eraw = [rott1[:, 0:384], rott1[:, 384:768]]
    qp0 = pool[:, 2 * KT:2 * KT + T]
    qp1 = pool[:, 3 * KT:3 * KT + T]
    kvr = sbuf("kvr", [128, 2 * 65])
    kpt = [sbuf(f"kpt{i}", [128, R]) for i in range(2)]
    wring = sbuf("wring", [128, 2 * 1024])
    w1ring = sbuf("w1ring", [128, 2 * 2048])
    w2ring = sbuf("w2ring", [128, 2816])
    stage = sbuf("stage", [128, 2 * 512], F32)
    fscr = sbuf("fscr", [16, 512], F32)
    lnb16 = fscr

    for j, vv in enumerate(CONSTVALS):
        nc.const_aps.register(F32, vv, consts[:, j:j + 1])

    OPS = {k: [] for k in ("tensor", "scalar", "vector", "gpsimd", "sync")}
    TQ, AQ, VQ, GQ, SQ = (OPS[k].append for k in ("tensor", "scalar", "vector",
                                                  "gpsimd", "sync"))

    def emit(Q, fn, incs=None):
        if incs is None:
            Q(fn)
            return
        ii = incs if isinstance(incs, list) else [incs]

        def g(e, fn=fn, ii=ii):
            inst = fn(e)
            for (hh, aa) in ii:
                inst = inst.then_inc(hh, aa)
            return inst
        Q(g)

    def inc(ctr, amt=1):
        ctr.n += amt
        return (ctr.h, amt)

    def wait(Q, ctr, v=None):
        vv = ctr.n if v is None else v
        Q(lambda e, h=ctr.h, vv=vv: e.wait_ge(h, vv))

    s_ld = Ctr("s_ld"); s_w = Ctr("s_w"); s_pe = Ctr("s_pe")
    s_act = Ctr("s_act"); s_dve = Ctr("s_dve"); s_gp = Ctr("s_gp")
    s_cc = Ctr("s_cc"); s_hd = Ctr("s_hd"); s_st = Ctr("s_st")
    s_wf = Ctr("s_wf")

    PS = [psm(f"p{i}", [128, 512]) for i in range(8)]
    s_pr = [Ctr(f"s_pr{i}") for i in range(8)]
    s_pf = [Ctr(f"s_pf{i}") for i in range(8)]

    class Bank:
        def __init__(self, i):
            self.i = i
            self.ap = PS[i]
            self.rd = s_pr[i]
            self.fr = s_pf[i]

        def acquire(self):
            if self.fr.n:
                wait(TQ, self.fr)

        def publish(self):
            return inc(self.rd)

        def cwait(self, Q):
            wait(Q, self.rd)

        def release(self):
            return inc(self.fr)

    banks = [Bank(i) for i in range(8)]

    for dst, src in ((consts, d_consts), (hsc, d_hs), (hidx_s, d_hidx),
                     (bqs, d_bq), (bks, d_bk), (bos, d_bo)):
        emit(SQ, lambda e, dd=dst, ss=src: e.dma_start(out=dd[:, :], in_=ss),
             inc(s_ld, 16))
    for dst, src in ((masks_s, d_masks), (tabAq, d_rotAq), (tabBq, d_rotBq), (tabAk, d_rotAk), (tabBk, d_rotBk),
                     (rfds, d_rfd), (ind16s, d_ind16),
                     (indhs, d_indh), (eyef, d_eye)):
        emit(GQ, lambda e, dd=dst, ss=src: e.dma_start(out=dd[:, :], in_=ss),
             inc(s_ld, 16))
    LD0 = s_ld.n
    emit(VQ, lambda e: e.memset(y[:, :], 0.0), inc(s_dve))
    emit(VQ, lambda e: e.memset(rowsA[:, :], 1.0), inc(s_dve))
    emit(VQ, lambda e: e.memset(rowsB[:, :], 1.0), inc(s_dve))
    emit(VQ, lambda e: e.memset(onesr[:, :], 1.0), inc(s_dve))
    emit(VQ, lambda e: e.memset(ones128[:, :], 1.0), inc(s_dve))
    emit(VQ, lambda e: e.memset(negones[:, :], -1.0), inc(s_dve))
    emit(VQ, lambda e: e.memset(v_tok[:, :], 0.0), inc(s_dve))
    emit(VQ, lambda e: e.memset(
        v_tok[:, :].rearrange("p (tt h d) -> p tt h d", h=H, d=65)[:, :, :, 64:65],
        1.0), inc(s_dve))
    MEMS0 = s_dve.n

    wait(GQ, s_ld, LD0)
    wait(GQ, s_dve, MEMS0)
    for b in range(NB):
        emit(GQ, lambda e, b=b: e.dma_start(
            out=yb(b)[:, :], in_=d_xT[b * 128:(b + 1) * 128, :]), inc(s_ld, 16))
    LDX = s_ld.n
    wait(TQ, s_ld, LDX)
    wait(AQ, s_ld, LDX)
    wait(TQ, s_dve, MEMS0)

    ln_chunks = [(i * 512, min(512, TTW - i * 512)) for i in range((TTW + 511) // 512)]
    bsx, bsq = banks[6], banks[7]
    qsq_gate = (s_pe, 0)
    for (off, w) in ln_chunks:
        bsx.acquire()
        bsq.acquire()
        for b in range(NB):
            wait(AQ, qsq_gate[0], qsq_gate[1])
            emit(AQ, lambda e, b=b, off=off, w=w: e.activation(
                qsq[:, 0:w], yb(b)[:, off:off + w], AF.Square, bias=0.0, scale=1.0),
                inc(s_act))
            wait(TQ, s_act)
            emit(TQ, lambda e, b=b, off=off, w=w: e.matmul(
                bsx.ap[0:1, 0:w], ones128[:, 0:1], yb(b)[:, off:off + w],
                start=(b == 0), stop=(b == NB - 1)),
                [bsx.publish()] if b == NB - 1 else None)
            if b == NB - 1:
                emit(TQ, lambda e, w=w, b=b: e.matmul(
                    bsq.ap[0:1, 0:w], ones128[:, 1:2], qsq[:, 0:w],
                    start=(b == 0), stop=True), bsq.publish())
                qsq_gate = (bsq.rd, bsq.rd.n)
            else:
                emit(TQ, lambda e, w=w, b=b: e.matmul(
                    bsq.ap[0:1, 0:w], ones128[:, 1:2], qsq[:, 0:w],
                    start=(b == 0), stop=False), inc(s_pe))
                qsq_gate = (s_pe, s_pe.n)
        bsx.cwait(VQ)
        bsq.cwait(VQ)
        emit(VQ, lambda e, w=w: e.tensor_scalar(
            fscr[0:1, 0:w], bsx.ap[0:1, 0:w], 1.0 / 1024.0, None, ALU.mult),
            bsx.release())
        emit(VQ, lambda e, w=w: e.tensor_scalar(
            stage[0:1, 0:w], bsq.ap[0:1, 0:w], 1.0 / 1024.0, None, ALU.mult),
            bsq.release())
        emit(VQ, lambda e, w=w: e.tensor_tensor(
            qsq[0:1, 0:w], fscr[0:1, 0:w], fscr[0:1, 0:w], ALU.mult))
        emit(VQ, lambda e, w=w: e.tensor_tensor(
            stage[0:1, 0:w], stage[0:1, 0:w], qsq[0:1, 0:w],
            ALU.subtract), inc(s_dve))
        wait(AQ, s_dve)
        emit(AQ, lambda e, w=w: e.activation(
            stage[0:1, 0:w], stage[0:1, 0:w], AF.Ln, bias=1e-5,
            scale=1.0))
        emit(AQ, lambda e, off=off, w=w: e.activation(
            rowsA[0:1, off:off + w], stage[0:1, 0:w], AF.Exp, bias=0.0,
            scale=-0.5), inc(s_act))
        wait(VQ, s_act)
        emit(VQ, lambda e, off=off, w=w: e.scalar_tensor_tensor(
            rowsB[0:1, off:off + w], fscr[0:1, 0:w], -1.0, rowsA[0:1, off:off + w],
            ALU.mult, ALU.mult), inc(s_dve))
    LNROWS_DVE = s_dve.n

    bpa, bpb = banks[4], banks[5]
    wait(TQ, s_dve, LNROWS_DVE)
    for b in range(NB):
        for (off, w) in ln_chunks:
            bpa.acquire()
            bpb.acquire()
            emit(TQ, lambda e, off=off, w=w: e.matmul(
                bpa.ap[:, 0:w], onesr[:, :], rowsA[0:1, off:off + w],
                start=True, stop=True), bpa.publish())
            emit(TQ, lambda e, off=off, w=w: e.matmul(
                bpb.ap[:, 0:w], onesr[:, :], rowsB[0:1, off:off + w],
                start=True, stop=True), bpb.publish())
            bpa.cwait(VQ)
            bpb.cwait(VQ)
            emit(VQ, lambda e, b=b, off=off, w=w: e.tensor_tensor(
                rott0[:, 0:w], yb(b)[:, off:off + w], bpa.ap[:, 0:w], ALU.mult),
                bpa.release())
            emit(VQ, lambda e, b=b, off=off, w=w: e.tensor_tensor(
                yb(b)[:, off:off + w], rott0[:, 0:w], bpb.ap[:, 0:w], ALU.add),
                bpb.release())
    LNDONE = (bpb.fr.h, bpb.fr.n)

    ygate = [(bpb.fr.h, bpb.fr.n)]

    for bk in banks:
        bk.gate = None

    def bk_acquire(bk, Q=TQ):
        if bk.gate is not None:
            Q(lambda e, g=bk.gate: e.wait_ge(g[0], g[1]))

    def bk_gate_now(bk, ctr):
        bk.gate = (ctr.h, ctr.n)

    def performer():
        afg = None
        for b in range(NB):
            slot = wring[:, (b % 2) * 1024:(b % 2 + 1) * 1024]
            g = getattr(performer, f"_wg{b % 2}", None)
            if g is not None:
                GQ(lambda e, g=g: e.wait_ge(g[0], g[1]))
            GQ(lambda e, g=ygate[0]: e.wait_ge(g[0], g[1]))
            emit(GQ, lambda e, b=b, slot=slot: e.dma_start(
                out=slot[:, :].rearrange("p (c f) -> p c f", f=128),
                in_=d_wkT[:, b * 128:(b + 1) * 128].rearrange(
                    "(c p) f -> p c f", p=128)), inc(s_ld, 16))
            wait(TQ, s_ld)
            TQ(lambda e, g=ygate[0]: e.wait_ge(g[0], g[1]))
            lastbk = None
            for ch in range(4):
                bk = banks[ch % 2]
                bk_acquire(bk)
                for cb in range(NB):
                    emit(TQ, lambda e, cb=cb, ch=ch, slot=slot, bk=bk: e.matmul(
                        bk.ap[:, :], slot[:, cb * 128:(cb + 1) * 128],
                        yb(cb)[:, HB + ch * 512:HB + (ch + 1) * 512],
                        start=(cb == 0), stop=(cb == NB - 1)),
                        bk.publish() if cb == NB - 1 else None)
                bk.cwait(VQ)
                emit(VQ, lambda e, b=b, ch=ch, bk=bk: e.tensor_scalar(
                    s1b(b)[:, ch * 512:(ch + 1) * 512], bk.ap[:, :],
                    bks[:, b:b + 1], None, ALU.add), inc(s_dve))
                bk_gate_now(bk, s_dve)
                lastbk = bk
            setattr(performer, f"_wg{b % 2}", (lastbk.rd.h, lastbk.rd.n))
            wait(AQ, s_dve)
            emit(AQ, lambda e, b=b: e.activation(
                qsq[:, 0:T], s1b(b)[:, :], AF.Square, bias=0.0, scale=1.0),
                inc(s_act))
            bk2 = banks[2]
            bk_acquire(bk2)
            wait(TQ, s_act)
            for tt in range(16):
                emit(TQ, lambda e, tt=tt, bk2=bk2: e.matmul(
                    bk2.ap[:, 2 * tt:2 * tt + 2], qsq[:, tt * 128:(tt + 1) * 128],
                    indhs[:, 0:2], start=True, stop=True),
                    bk2.publish() if tt == 15 else None)
            bk2.cwait(VQ)
            emit(VQ, lambda e, b=b, bk2=bk2: e.tensor_copy(
                n2tok[:, b * 32:(b + 1) * 32], bk2.ap[:, 0:32]), inc(s_dve))
            bk_gate_now(bk2, s_dve)
        KPROJ_DONE = (s_dve.h, s_dve.n)

        wvres = pool[:, 0:8 * 1024]
        GQ(lambda e, g=KPROJ_DONE: e.wait_ge(g[0], g[1]))
        emit(GQ, lambda e: e.dma_start(out=bvr[:, :], in_=d_bvrow), inc(s_ld, 16))
        for cb in range(NB):
            emit(GQ, lambda e, cb=cb: e.dma_start(
                out=wvres[:, cb * 1024:(cb + 1) * 1024],
                in_=d_wvT[cb * 128:(cb + 1) * 128, :]), inc(s_ld, 16))
        wait(TQ, s_ld)
        for tt in range(16):
            for fc in range(2):
                bk = banks[(tt * 2 + fc) % 2]
                bk_acquire(bk)
                for cb in range(NB):
                    emit(TQ, lambda e, cb=cb, tt=tt, fc=fc, bk=bk: e.matmul(
                        bk.ap[:, :], yb(cb)[:, HB + 128 * tt:HB + 128 * tt + 128],
                        wvres[:, cb * 1024 + fc * 512:cb * 1024 + (fc + 1) * 512],
                        start=(cb == 0), stop=False))
                emit(TQ, lambda e, fc=fc, bk=bk: e.matmul(
                    bk.ap[:, :], onesr[:, :], bvr[0:1, fc * 512:(fc + 1) * 512],
                    start=False, stop=True), bk.publish())
                bk.cwait(VQ)
                emit(VQ, lambda e, tt=tt, fc=fc, bk=bk: e.tensor_copy(
                    vt(tt).rearrange("p (h d) -> p h d", d=65)[:, fc * 8:(fc + 1) * 8, 0:64],
                    bk.ap[:, :].rearrange("p (h d) -> p h d", d=64)), inc(s_dve))
                bk_gate_now(bk, s_dve)
        V_DONE = (s_dve.h, s_dve.n)

        TQ(lambda e, g=V_DONE: e.wait_ge(g[0], g[1]))
        for h in range(H):
            b, hh = h // 2, h % 2
            kb4, kb5 = banks[4], banks[5]
            bk_acquire(kb4)
            bk_acquire(kb5)
            kp_gate = [None, None]
            for tt in range(16):
                bk = banks[3 if tt % 2 == 0 else 6]
                bk_acquire(bk)
                emit(TQ, lambda e, b=b, hh=hh, h=h, tt=tt, bk=bk: e.matmul(
                    bk.ap[:, 0:R], s1b(b)[hh * 64:(hh + 1) * 64, tt * 128:(tt + 1) * 128],
                    rfds[hh * 64:(hh + 1) * 64, h * R:(h + 1) * R],
                    start=True, stop=True), bk.publish())
                bk.cwait(AQ)
                if kp_gate[tt % 2] is not None:
                    AQ(lambda e, g=kp_gate[tt % 2]: e.wait_ge(g[0], g[1]))
                emit(AQ, lambda e, b=b, hh=hh, tt=tt, bk=bk: e.activation(
                    kpt[tt % 2][:, :], bk.ap[:, 0:R], AF.Exp,
                    bias=n2tok[:, b * 32 + 2 * tt + hh:b * 32 + 2 * tt + hh + 1],
                    scale=1.0), inc(s_act))
                bk_gate_now(bk, s_act)
                wait(TQ, s_act)
                emit(TQ, lambda e, h=h, tt=tt, kb4=kb4: e.matmul(
                    kb4.ap[:, 0:65], kpt[tt % 2][:, 0:128],
                    vt(tt)[:, h * 65:(h + 1) * 65],
                    start=(tt == 0), stop=(tt == 15)),
                    kb4.publish() if tt == 15 else None)
                emit(TQ, lambda e, h=h, tt=tt, kb5=kb5: e.matmul(
                    kb5.ap[:, 0:65], kpt[tt % 2][:, 128:256],
                    vt(tt)[:, h * 65:(h + 1) * 65],
                    start=(tt == 0), stop=(tt == 15)),
                    kb5.publish() if tt == 15 else None)
                kp_gate[tt % 2] = (kb5.rd.h, kb5.rd.n)
            kb4.cwait(VQ)
            kb5.cwait(VQ)
            emit(VQ, lambda e, kb4=kb4: e.tensor_copy(
                stage[:, 0:65], kb4.ap[:, 0:65]), inc(s_dve))
            bk_gate_now(kb4, s_dve)
            emit(VQ, lambda e, kb5=kb5: e.tensor_copy(
                stage[:, 65:130], kb5.ap[:, 0:65]), inc(s_dve))
            bk_gate_now(kb5, s_dve)
            wait(SQ, s_dve)
            emit(SQ, lambda e, h=h: e.dma_start(
                out=d_kvin.tensor.ap()[:, h * 130:(h + 1) * 130],
                in_=stage[:, 0:130]), inc(s_hd, 16))
            wait(VQ, s_hd)

        wait(GQ, s_hd)
        emit(GQ, lambda e: e.collective_compute(
            "AllReduce", ALU.add, replica_groups=[list(range(NCORES))],
            ins=[d_kvin.tensor.ap().opt()], outs=[d_kvout.tensor.ap().opt()]),
            inc(s_cc))
        wait(GQ, s_cc)

        for b in range(NB):
            slot = wring[:, (b % 2) * 1024:(b % 2 + 1) * 1024]
            g = getattr(performer, f"_wg{b % 2}", None)
            if g is not None:
                GQ(lambda e, g=g: e.wait_ge(g[0], g[1]))
            emit(GQ, lambda e, b=b, slot=slot: e.dma_start(
                out=slot[:, :].rearrange("p (c f) -> p c f", f=128),
                in_=d_wqT[:, b * 128:(b + 1) * 128].rearrange(
                    "(c p) f -> p c f", p=128)), inc(s_ld, 16))
            wait(TQ, s_ld)
            lastbk = None
            for ch in range(4):
                bk = banks[ch % 2]
                bk_acquire(bk)
                for cb in range(NB):
                    emit(TQ, lambda e, cb=cb, ch=ch, slot=slot, bk=bk: e.matmul(
                        bk.ap[:, :], slot[:, cb * 128:(cb + 1) * 128],
                        yb(cb)[:, HB + ch * 512:HB + (ch + 1) * 512],
                        start=(cb == 0), stop=(cb == NB - 1)),
                        bk.publish() if cb == NB - 1 else None)
                bk.cwait(VQ)
                emit(VQ, lambda e, b=b, ch=ch, bk=bk: e.tensor_scalar(
                    qfeat[:, ch * 512:(ch + 1) * 512], bk.ap[:, :],
                    bqs[:, b:b + 1], None, ALU.add), inc(s_dve))
                bk_gate_now(bk, s_dve)
                lastbk = bk
            setattr(performer, f"_wg{b % 2}", (lastbk.rd.h, lastbk.rd.n))
            wait(AQ, s_dve)
            emit(AQ, lambda e: e.activation(
                qsq[:, 0:T], qfeat[:, 0:T], AF.Square, bias=0.0, scale=1.0),
                inc(s_act))
            bk2 = banks[2]
            bk_acquire(bk2)
            wait(TQ, s_act)
            for tt in range(16):
                emit(TQ, lambda e, tt=tt, bk2=bk2: e.matmul(
                    bk2.ap[:, 2 * tt:2 * tt + 2], qsq[:, tt * 128:(tt + 1) * 128],
                    indhs[:, 2:4], start=True, stop=True),
                    bk2.publish() if tt == 15 else None)
            bk2.cwait(VQ)
            emit(VQ, lambda e, b=b, bk2=bk2: e.tensor_copy(
                n2tok[:, b * 32:(b + 1) * 32], bk2.ap[:, 0:32]), inc(s_dve))
            bk_gate_now(bk2, s_dve)
            N2Q = (s_dve.h, s_dve.n)
            for hh in range(2):
                h = 2 * b + hh
                bk3 = banks[3]
                bk_acquire(bk3)
                TQ(lambda e, g=N2Q: e.wait_ge(g[0], g[1]))
                emit(TQ, lambda e, b=b, hh=hh, bk3=bk3: e.transpose(
                    bk3.ap[0:16, 0:128],
                    n2tok[:, b * 32 + hh:b * 32 + 32:2],
                    eyef[:, :]), bk3.publish())
                bk3.cwait(VQ)
                emit(VQ, lambda e, bk3=bk3: e.tensor_copy(
                    trow[0:16, 0:128], bk3.ap[0:16, 0:128]), inc(s_dve))
                bk_gate_now(bk3, s_dve)
                wait(GQ, s_dve)
                emit(GQ, lambda e: e.dma_start(
                    out=wrow[0:1, 0:T], in_=trow[0:16, 0:128]), inc(s_hd, 16))
                wait(TQ, s_hd)
                for rh in range(2):
                    for ch in range(4):
                        bkq = banks[4 + (ch % 2)]
                        bk_acquire(bkq)
                        emit(TQ, lambda e, hh=hh, h=h, rh=rh, ch=ch, bkq=bkq: e.matmul(
                            bkq.ap[:, :],
                            rfds[hh * 64:(hh + 1) * 64,
                                 h * R + rh * 128:h * R + rh * 128 + 128],
                            qfeat[hh * 64:(hh + 1) * 64, ch * 512:(ch + 1) * 512],
                            start=True, stop=False))
                        emit(TQ, lambda e, rh=rh, ch=ch, bkq=bkq: e.matmul(
                            bkq.ap[:, :], negones[:, :],
                            wrow[0:1, ch * 512:(ch + 1) * 512],
                            start=False, stop=True), bkq.publish())
                        bkq.cwait(AQ)
                        dstq = [qp0, qp1][rh]
                        emit(AQ, lambda e, dstq=dstq, ch=ch, bkq=bkq: e.activation(
                            dstq[:, ch * 512:(ch + 1) * 512], bkq.ap[:, :], AF.Exp,
                            bias=0.0, scale=1.0), inc(s_act))
                        bk_gate_now(bkq, s_act)
                wait(GQ, s_cc)
                g = getattr(performer, "_kvg", None)
                if g is not None:
                    GQ(lambda e, g=g: e.wait_ge(g[0], g[1]))
                emit(GQ, lambda e, h=h: e.dma_start(
                    out=kvr[:, :], in_=d_kvout.tensor.ap()[:, h * 130:(h + 1) * 130]),
                    inc(s_ld, 16))
                wait(TQ, s_ld)
                wait(TQ, s_act)
                for ch in range(4):
                    bka = banks[6 + (ch % 2)]
                    bk_acquire(bka)
                    emit(TQ, lambda e, ch=ch, bka=bka: e.matmul(
                        bka.ap[0:65, :], kvr[:, 0:65],
                        qp0[:, ch * 512:(ch + 1) * 512], start=True, stop=False))
                    emit(TQ, lambda e, ch=ch, bka=bka: e.matmul(
                        bka.ap[0:65, :], kvr[:, 65:130],
                        qp1[:, ch * 512:(ch + 1) * 512], start=False, stop=True),
                        bka.publish())
                    bka.cwait(VQ)
                    emit(VQ, lambda e, bka=bka: e.tensor_scalar(
                        fscr[0:1, 0:512], bka.ap[64:65, 0:512], 1e-6, None,
                        ALU.add), inc(s_dve))
                    wait(VQ, s_dve)

                    def _recipb(e, bka=bka):
                        with nc.allow_low_precision(reason="bf16 1/norm operand"):
                            return e.reciprocal(rrow[0:1, 0:512], fscr[0:1, 0:512])
                    emit(VQ, _recipb, inc(s_dve))
                    emit(VQ, lambda e, bka=bka: e.tensor_copy(
                        khat[0:64, 0:512], bka.ap[0:64, 0:512]), inc(s_dve))
                    bk_gate_now(bka, s_dve)
                    bkb = banks[2]
                    bk_acquire(bkb)
                    wait(TQ, s_dve)
                    emit(TQ, lambda e, bkb=bkb: e.matmul(
                        bkb.ap[0:64, 0:512], onesr[0:1, 0:64], rrow[0:1, 0:512],
                        start=True, stop=True), bkb.publish())
                    bkb.cwait(VQ)
                    emit(VQ, lambda e, b=b, hh=hh, ch=ch, bkb=bkb: e.tensor_tensor(
                        s1b(b)[hh * 64:(hh + 1) * 64, ch * 512:(ch + 1) * 512],
                        khat[0:64, 0:512], bkb.ap[0:64, 0:512], ALU.mult),
                        inc(s_dve))
                    bk_gate_now(bkb, s_dve)
                    performer._kvg = (bka.rd.h, bka.rd.n)
        PAF_DONE = (s_dve.h, s_dve.n)

        TQ(lambda e, g=PAF_DONE: e.wait_ge(g[0], g[1]))
        for ob in range(NB):
            slot = wring[:, (ob % 2) * 1024:(ob % 2 + 1) * 1024]
            g = getattr(performer, f"_wg{ob % 2}", None)
            if g is not None:
                GQ(lambda e, g=g: e.wait_ge(g[0], g[1]))
            emit(GQ, lambda e, ob=ob, slot=slot: e.dma_start(
                out=slot[:, :].rearrange("p (c f) -> p c f", f=128),
                in_=d_woT[:, ob * 128:(ob + 1) * 128].rearrange(
                    "(c p) f -> p c f", p=128)), inc(s_ld, 16))
            wait(TQ, s_ld)
            lastbk = None
            for tc in range(4):
                bk = banks[tc % 2]
                bk_acquire(bk)
                for cb in range(NB):
                    emit(TQ, lambda e, cb=cb, tc=tc, slot=slot, bk=bk: e.matmul(
                        bk.ap[:, :], slot[:, cb * 128:(cb + 1) * 128],
                        s1b(cb)[:, tc * 512:(tc + 1) * 512],
                        start=(cb == 0), stop=(cb == NB - 1)),
                        bk.publish() if cb == NB - 1 else None)
                bk.cwait(VQ)
                wait(VQ, s_st)
                emit(VQ, lambda e, ob=ob, tc=tc, bk=bk: e.tensor_scalar(
                    stage[:, 512:1024], bk.ap[:, :], bos[:, ob:ob + 1], None,
                    ALU.add), inc(s_dve))
                bk_gate_now(bk, s_dve)
                wait(SQ, s_dve)
                emit(SQ, lambda e, ob=ob, tc=tc: e.dma_start(
                    out=d_ssp[ob * 128:(ob + 1) * 128, tc * 512:(tc + 1) * 512],
                    in_=stage[:, 512:1024]), inc(s_st, 16))
                lastbk = bk
            setattr(performer, f"_wg{ob % 2}", (lastbk.rd.h, lastbk.rd.n))

    def load_layer_consts(l):
        SQ(lambda e, g=ygate[0]: e.wait_ge(g[0], g[1]))
        GQ(lambda e, g=ygate[0]: e.wait_ge(g[0], g[1]))
        emit(SQ, lambda e, l=l: e.dma_start(out=b1s[:, :], in_=d_b1c[l]), inc(s_ld, 16))
        emit(SQ, lambda e, l=l: e.dma_start(out=b2s[:, :], in_=d_b2c[l]), inc(s_ld, 16))
        emit(GQ, lambda e, l=l: e.dma_start(
            out=srow_s[:, :], in_=d_srow[0:1, l * 128:(l + 1) * 128]), inc(s_ld, 16))
        emit(GQ, lambda e, l=l: e.dma_start(
            out=glbt[:, :].rearrange("r (b p) -> r b p", p=128),
            in_=d_glb[l].rearrange("b r p -> r b p")), inc(s_ld, 16))
        emit(GQ, lambda e, l=l: e.dma_start(
            out=indq[:, :].rearrange("h (b p) -> h b p", p=128),
            in_=d_ind[l, 0].rearrange("b h p -> h b p")), inc(s_ld, 16))
        emit(GQ, lambda e, l=l: e.dma_start(
            out=indk[:, :].rearrange("h (b p) -> h b p", p=128),
            in_=d_ind[l, 1].rearrange("b h p -> h b p")), inc(s_ld, 16))

    def compute_cbias():
        bk3 = banks[3]
        bk_acquire(bk3)
        wait(TQ, s_ld)
        emit(TQ, lambda e: e.matmul(bk3.ap[:, 0:128], onesr[:, :],
             srow_s[0:1, 0:128], start=True, stop=True), bk3.publish())
        bk3.cwait(VQ)
        wait(VQ, s_act)
        emit(VQ, lambda e: e.reduce_max(cbias[:, 1:2], bk3.ap[:, 0:64], axis=AX.X,
             apply_absolute_value=True), inc(s_dve))
        emit(VQ, lambda e: e.reduce_max(stage[:, 0:1], bk3.ap[:, 64:128], axis=AX.X,
             apply_absolute_value=True), inc(s_dve))
        bk_gate_now(bk3, s_dve)
        wait(VQ, s_dve)
        emit(VQ, lambda e: e.tensor_tensor(stage[:, 1:2], cbias[:, 1:2],
             stage[:, 0:1], ALU.mult), inc(s_dve))
        wait(VQ, s_dve)
        emit(VQ, lambda e: e.tensor_scalar(cbias[:, 0:1], stage[:, 1:2], -8.0, None,
             ALU.mult), inc(s_dve))
        if "cbdbg" in dump_outs:
            dd = dump_outs["cbdbg"]
            wait(GQ, s_dve)
            emit(GQ, lambda e, dd=dd: e.dma_start(
                out=dd[0:128, 0:2], in_=stage[:, 0:2]), inc(s_hd, 16))
            emit(GQ, lambda e, dd=dd: e.dma_start(
                out=dd[0:128, 2:4], in_=cbias[:, 0:2]), inc(s_hd, 16))
            emit(GQ, lambda e, dd=dd: e.dma_start(
                out=dd[128:129, 0:128], in_=srow_s[:, :]), inc(s_hd, 16))
            wait(GQ, s_hd)
            for Qx in (TQ, VQ, AQ):
                wait(Qx, s_hd)

    def sim_range(kc):
        r0 = max(0, 128 * (kc - 2))
        r1 = min(T, 128 * kc + 128)
        return r0, r1 - r0

    def mask_view(kc, M):
        if kc == 0:
            return masks_s[:, 384:384 + 128]
        if kc == 1:
            return masks_s[:, 512:512 + 256]
        return masks_s[:, 0:M]

    def inner_layer(l, first=False):
        load_layer_consts(l)
        compute_cbias()
        wvres = pool[:, 0:8 * 1024]
        GQ(lambda e, g=ygate[0]: e.wait_ge(g[0], g[1]))
        for cb in range(NB):
            emit(GQ, lambda e, l=l, cb=cb: e.dma_start(
                out=wvres[:, cb * 1024:(cb + 1) * 1024].rearrange(
                    "p (f c) -> p f c", c=128),
                in_=d_wqkv[l, 16:24, cb].rearrange("f p c -> p f c")), inc(s_ld, 16))
        wait(TQ, s_ld)
        TQ(lambda e, g=ygate[0]: e.wait_ge(g[0], g[1]))
        for tt in range(NKT):
            for fc in range(2):
                bk = banks[(tt * 2 + fc) % 2]
                bk_acquire(bk)
                for cb in range(NB):
                    off = (2 if cb < 4 else 1) + 128 * tt
                    emit(TQ, lambda e, cb=cb, off=off, fc=fc, bk=bk: e.matmul(
                        bk.ap[:, :], yb(cb)[:, off:off + 128],
                        wvres[:, cb * 1024 + fc * 512:cb * 1024 + (fc + 1) * 512],
                        start=(cb == 0), stop=(cb == NB - 1)),
                        bk.publish() if cb == NB - 1 else None)
                bk.cwait(VQ)
                emit(VQ, lambda e, tt=tt, fc=fc, bk=bk: e.tensor_copy(
                    vt(tt).rearrange("p (h d) -> p h d", d=65)[:, fc * 8:(fc + 1) * 8, 0:64],
                    bk.ap[:, :].rearrange("p (h d) -> p h d", d=64)), inc(s_dve))
                bk_gate_now(bk, s_dve)
        VDONE = (s_dve.h, s_dve.n)

        for b in range(NB):
            prep_gate = None
            for (tens, wfb, TW, dst_hat, indt, tabA_, tabB_) in (
                    ("q", b, T, qhat, indq, tabAq, tabBq),
                    ("k", 8 + b, KT, khat, indk, tabAk, tabBk)):
                slotpar = 0 if tens == "q" else 1
                slot = wring[:, slotpar * 1024:(slotpar + 1) * 1024]
                g = getattr(inner_layer, f"_wg{slotpar}", None)
                if g is not None:
                    GQ(lambda e, g=g: e.wait_ge(g[0], g[1]))
                emit(GQ, lambda e, l=l, wfb=wfb, slot=slot: e.dma_start(
                    out=slot[:, :].rearrange("p (c f) -> p c f", f=128),
                    in_=d_wqkv[l, wfb].rearrange("c p f -> p c f")),
                    inc(s_ld, 16))
                wait(TQ, s_ld)
                nch = (TW + 511) // 512
                lastbk = None
                for ch in range(nch):
                    off = ch * 512
                    w = min(512, TW - off)
                    bk = banks[ch % 2]
                    bk_acquire(bk)
                    for cb in range(NB):
                        if tens == "q":
                            src_off = HB + off + (0 if cb < 4 else -1)
                        else:
                            src_off = 2 + off + (0 if cb < 4 else -1)
                        emit(TQ, lambda e, cb=cb, src_off=src_off, w=w, slot=slot,
                             bk=bk: e.matmul(
                                 bk.ap[:, 0:w], slot[:, cb * 128:(cb + 1) * 128],
                                 yb(cb)[:, src_off:src_off + w],
                                 start=(cb == 0), stop=(cb == NB - 1)),
                             bk.publish() if cb == NB - 1 else None)
                    bk.cwait(VQ)
                    emit(VQ, lambda e, off=off, w=w, bk=bk: e.tensor_copy(
                        qfeat[:, off:off + w], bk.ap[:, 0:w]), inc(s_dve))
                    bk_gate_now(bk, s_dve)
                    lastbk = bk
                setattr(inner_layer, f"_wg{slotpar}", (lastbk.rd.h, lastbk.rd.n))
                evac_done = (s_dve.h, s_dve.n)
                AQ(lambda e, g=evac_done: e.wait_ge(g[0], g[1]))
                emit(AQ, lambda e, TW=TW: e.activation(
                    qsq[:, 0:TW], qfeat[:, 0:TW], AF.Square, bias=0.0, scale=1.0),
                    inc(s_act))
                wait(TQ, s_act)
                bk3 = banks[3]
                norm_done = None
                for ch in range(nch):
                    off = ch * 512
                    w = min(512, TW - off)
                    bk_acquire(bk3)
                    emit(TQ, lambda e, b=b, off=off, w=w: e.matmul(
                        bk3.ap[0:16, 0:w], ind16s[:, b * 16:(b + 1) * 16],
                        qsq[:, off:off + w], start=True, stop=True), bk3.publish())
                    bk3.cwait(AQ)
                    emit(AQ, lambda e, w=w: e.activation(
                        lnb16[0:16, 0:w], bk3.ap[0:16, 0:w], AF.Ln, bias=1e-17,
                        scale=1.0))
                    emit(AQ, lambda e, w=w: e.activation(
                        rnorm[0:16, 0:w], lnb16[0:16, 0:w], AF.Exp, bias=0.0,
                        scale=-0.5), inc(s_act))
                    bk_gate_now(bk3, s_act)
                    bk2 = banks[2]
                    bk_acquire(bk2)
                    wait(TQ, s_act)
                    emit(TQ, lambda e, b=b, off=off, w=w, indt=indt: e.matmul(
                        bk2.ap[:, 0:w], indt[:, b * 128:(b + 1) * 128],
                        rnorm[0:16, 0:w], start=True, stop=True), bk2.publish())
                    bk2.cwait(VQ)
                    emit(VQ, lambda e, off=off, w=w: e.tensor_tensor(
                        rott0[:, off:off + w], qfeat[:, off:off + w],
                        bk2.ap[:, 0:w], ALU.mult), inc(s_dve))
                    bk_gate_now(bk2, s_dve)
                emit(VQ, lambda e, TW=TW, tabA_=tabA_: e.tensor_tensor(
                    qsq[:, 0:TW], rott0[:, 0:TW], tabA_[:, 0:TW], ALU.mult))
                for (dst0, src0) in ((0, 32), (32, 0), (64, 96), (96, 64)):
                    emit(VQ, lambda e, dst0=dst0, src0=src0, TW=TW: e.tensor_copy(
                        rott1[dst0:dst0 + 32, 0:TW], rott0[src0:src0 + 32, 0:TW]))
                emit(VQ, lambda e, TW=TW, tabB_=tabB_: e.tensor_tensor(
                    rott1[:, 0:TW], rott1[:, 0:TW], tabB_[:, 0:TW], ALU.mult))
                emit(VQ, lambda e, TW=TW, dst_hat=dst_hat: e.tensor_tensor(
                    dst_hat[:, 0:TW], qsq[:, 0:TW], rott1[:, 0:TW], ALU.add),
                    inc(s_dve))
                prep_gate = (s_dve.h, s_dve.n)

            if b == 0 and f"pr{l}" in dump_outs:
                dd = dump_outs[f"pr{l}"]
                GQ(lambda e, g=prep_gate: e.wait_ge(g[0], g[1]))
                emit(GQ, lambda e, dd=dd: e.dma_start(
                    out=dd[0:128, 0:T], in_=qhat[:, :]), inc(s_hd, 16))
                emit(GQ, lambda e, dd=dd: e.dma_start(
                    out=dd[128:256, 0:2048], in_=khat[:, 0:2048]), inc(s_hd, 16))
                emit(GQ, lambda e, dd=dd: e.dma_start(
                    out=dd[256:384, 0:2], in_=cbias[:, :]), inc(s_hd, 16))
                emit(GQ, lambda e, dd=dd: e.dma_start(
                    out=dd[384:512, 0:1040],
                    in_=vt(4)[:, 0:1040]),
                    inc(s_hd, 16))
                wait(GQ, s_hd)
            TQ(lambda e, g=prep_gate: e.wait_ge(g[0], g[1]))
            TQ(lambda e, g=VDONE: e.wait_ge(g[0], g[1]))
            for hh in range(2):
                h = 2 * b + hh
                hb = hh * 64
                eslot_gate = [None] * 6
                eraw_gate = [None] * 2
                for kc in range(NKT):
                    r0, M = sim_range(kc)
                    bk = banks[4 + (kc % 2)]
                    bk_acquire(bk)
                    emit(TQ, lambda e, hb=hb, kc=kc, r0=r0, M=M, bk=bk: e.matmul(
                        bk.ap[0:128, 0:M], khat[hb:hb + 64, kc * 128:(kc + 1) * 128],
                        qhat[hb:hb + 64, r0:r0 + M], start=True, stop=True),
                        bk.publish())
                    er = Eraw[kc % 2]
                    if eraw_gate[kc % 2] is not None:
                        AQ(lambda e, g=eraw_gate[kc % 2]: e.wait_ge(g[0], g[1]))
                    bk.cwait(AQ)
                    emit(AQ, lambda e, er=er, M=M, bk=bk: e.activation(
                        er[:, 0:M], bk.ap[:, 0:M], AF.Exp, bias=cbias[:, 0:1],
                        scale=8.0), inc(s_act))
                    bk_gate_now(bk, s_act)
                    es = Etile[kc % 6]
                    if eslot_gate[kc % 6] is not None:
                        VQ(lambda e, g=eslot_gate[kc % 6]: e.wait_ge(g[0], g[1]))
                    wait(VQ, s_act)
                    emit(VQ, lambda e, es=es, er=er, kc=kc, M=M: e.tensor_tensor(
                        es[:, 0:M], er[:, 0:M], mask_view(kc, M), ALU.mult),
                        inc(s_dve))
                    eraw_gate[kc % 2] = (s_dve.h, s_dve.n)
                    Eready = (s_dve.h, s_dve.n)
                    if b == 0 and hh == 0 and kc == 4 and "eav" in dump_outs:
                        dd = dump_outs["eav"]
                        GQ(lambda e, g=Eready: e.wait_ge(g[0], g[1]))
                        emit(GQ, lambda e, dd=dd: e.dma_start(
                            out=dd[0:128, 0:384], in_=Etile[4][:, :]), inc(s_hd, 16))
                        emit(GQ, lambda e, dd=dd: e.dma_start(
                            out=dd[128:256, 0:384], in_=Eraw[0][:, :]), inc(s_hd, 16))
                        wait(GQ, s_hd)
                        for Qx in (TQ, VQ, AQ):
                            wait(Qx, s_hd)
                    for g_ in range(4):
                        if min(17, 4 * g_ + 5) != kc:
                            continue
                        avb = banks[6 + (g_ % 2)]
                        bk_acquire(avb)
                        TQ(lambda e, g=Eready: e.wait_ge(g[0], g[1]))
                        kcs = [k2 for k2 in range(NKT)
                               if max(0, 128 * (k2 - 2)) < 512 * (g_ + 1)
                               and min(T, 128 * k2 + 128) > 512 * g_]
                        for j, k2 in enumerate(kcs):
                            rr0 = max(0, 128 * (k2 - 2))
                            qlo = max(512 * g_, rr0)
                            qhi = min(512 * (g_ + 1), min(T, 128 * k2 + 128))
                            emit(TQ, lambda e, h=h, k2=k2, j=j, qlo=qlo, qhi=qhi,
                                 rr0=rr0, g_=g_, avb=avb: e.matmul(
                                     avb.ap[0:65, qlo - 512 * g_:qhi - 512 * g_],
                                     vt(k2)[:, h * 65:(h + 1) * 65],
                                     Etile[k2 % 6][:, qlo - rr0:qhi - rr0],
                                     start=(j == 0), stop=(j == len(kcs) - 1)),
                                 avb.publish() if j == len(kcs) - 1 else None)
                        for k2 in kcs:
                            eslot_gate[k2 % 6] = (avb.rd.h, avb.rd.n)
                        avb.cwait(VQ)
                        bk2 = banks[2]
                        if bk2.gate is not None:
                            VQ(lambda e, g=bk2.gate: e.wait_ge(g[0], g[1]))
                        def _recip(e, avb=avb):
                            with nc.allow_low_precision(
                                    reason="bf16 1/rowsum as matmul operand"):
                                return e.reciprocal(rrow[0:1, 0:512],
                                                    avb.ap[64:65, 0:512])
                        emit(VQ, _recip, inc(s_dve))
                        emit(VQ, lambda e, avb=avb: e.tensor_copy(
                            rott0[0:64, 0:512], avb.ap[0:64, 0:512]), inc(s_dve))
                        bk_gate_now(avb, s_dve)
                        bk_acquire(bk2)
                        wait(TQ, s_dve)
                        emit(TQ, lambda e, bk2=bk2: e.matmul(
                            bk2.ap[0:64, 0:512], onesr[0:1, 0:64], rrow[0:1, 0:512],
                            start=True, stop=True), bk2.publish())
                        bk2.cwait(VQ)
                        emit(VQ, lambda e, b=b, hb=hb, g_=g_, bk2=bk2: e.tensor_tensor(
                            s1b(b)[hb:hb + 64, g_ * 512:(g_ + 1) * 512],
                            rott0[0:64, 0:512], bk2.ap[0:64, 0:512], ALU.mult),
                            inc(s_dve))
                        bk_gate_now(bk2, s_dve)
                        if b == 0 and hh == 0 and g_ == 0 and "eav" in dump_outs:
                            dd = dump_outs["eav"]
                            wait(GQ, s_dve)
                            emit(GQ, lambda e, dd=dd: e.dma_start(
                                out=dd[256:320, 0:512], in_=rott0[0:64, 0:512]),
                                inc(s_hd, 16))
                            emit(GQ, lambda e, dd=dd: e.dma_start(
                                out=dd[320:321, 0:512], in_=rrow[0:1, 0:512]),
                                inc(s_hd, 16))
                            wait(GQ, s_hd)
                            for Qx in (TQ, VQ, AQ):
                                wait(Qx, s_hd)
        AFDONE = (s_dve.h, s_dve.n)

        TQ(lambda e, g=AFDONE: e.wait_ge(g[0], g[1]))
        for ob in range(NB):
            slot = wring[:, (ob % 2) * 1024:((ob % 2) + 1) * 1024]
            g = getattr(inner_layer, f"_wg{ob % 2}", None)
            if g is not None:
                GQ(lambda e, g=g: e.wait_ge(g[0], g[1]))
            emit(GQ, lambda e, l=l, ob=ob, slot=slot: e.dma_start(
                out=slot[:, :].rearrange("p (c f) -> p c f", f=128),
                in_=d_wo[l, ob].rearrange("c p f -> p c f")),
                inc(s_ld, 16))
            wait(TQ, s_ld)
            lastbk = None
            for tc in range(4):
                bk = banks[tc % 2]
                bk_acquire(bk)
                for cb in range(NB):
                    emit(TQ, lambda e, cb=cb, tc=tc, slot=slot, bk=bk: e.matmul(
                        bk.ap[:, :], slot[:, cb * 128:(cb + 1) * 128],
                        s1b(cb)[:, tc * 512:(tc + 1) * 512],
                        start=(cb == 0), stop=(cb == NB - 1)),
                        bk.publish() if cb == NB - 1 else None)
                bk.cwait(VQ)
                emit(VQ, lambda e, ob=ob, tc=tc, bk=bk: e.tensor_tensor(
                    yb(ob)[:, HB + tc * 512:HB + (tc + 1) * 512],
                    yb(ob)[:, HB + tc * 512:HB + (tc + 1) * 512],
                    bk.ap[:, :], ALU.add), inc(s_dve))
                bk_gate_now(bk, s_dve)
                lastbk = bk
            setattr(inner_layer, f"_wg{ob % 2}", (lastbk.rd.h, lastbk.rd.n))
        YA_DONE = (s_dve.h, s_dve.n)
        if f"ya{l}" in dump_outs or f"af{l}" in dump_outs:
            for nm_, srcfn in ((f"ya{l}", lambda b_: yb(b_)[:, HB:]),
                               (f"af{l}", lambda b_: s1b(b_)[:, :])):
                if nm_ not in dump_outs:
                    continue
                dd = dump_outs[nm_]
                gg = YA_DONE if nm_.startswith("ya") else AFDONE
                GQ(lambda e, g=gg: e.wait_ge(g[0], g[1]))
                for b_ in range(NB):
                    emit(GQ, lambda e, b_=b_, dd=dd, srcfn=srcfn: e.dma_start(
                        out=dd[b_ * 128:(b_ + 1) * 128, :], in_=srcfn(b_)),
                        inc(s_hd, 16))
            wait(GQ, s_hd)
            for Qx in (TQ, VQ, AQ, SQ):
                wait(Qx, s_hd)

        VQ(lambda e, g=YA_DONE: e.wait_ge(g[0], g[1]))
        emit(VQ, lambda e: e.tensor_copy(
            tcol[:, 0:NB],
            y[:, :].rearrange("p (b t) -> p b t", t=TTW)[:, :, TTW - 1:TTW]
            .rearrange("p b t -> p (b t)")), inc(s_dve))
        wait(GQ, s_dve)
        emit(GQ, lambda e: e.dma_start(
            out=d_tbin.tensor.ap(), in_=tcol[:, 0:NB]), inc(s_hd, 16))
        wait(GQ, s_hd)
        emit(GQ, lambda e: e.collective_compute(
            "AllGather", ALU.bypass, replica_groups=[list(range(NCORES))],
            ins=[d_tbin.tensor.ap().opt()], outs=[d_tbout.tensor.ap().opt()]),
            inc(s_cc))
        wait(GQ, s_cc)
        emit(GQ, lambda e: e.indirect_dma_start(
            out=tcol[:, NB:2 * NB], out_offset=None,
            in_=d_tbout.tensor.ap(),
            in_offset=bass.IndirectOffsetOnAxis(ap=hidx_s[:, :], axis=0)),
            inc(s_hd, 16))
        wait(VQ, s_hd)
        emit(VQ, lambda e: e.tensor_scalar(
            y[:, :].rearrange("p (b t) -> p b t", t=TTW)[:, :, HB - 1:HB]
            .rearrange("p b t -> p (b t)"),
            tcol[:, NB:2 * NB], hsc[:, 0:1], None, ALU.mult), inc(s_dve))
        TAIL_DONE = (s_dve.h, s_dve.n)

        TQ(lambda e, g=TAIL_DONE: e.wait_ge(g[0], g[1]))
        AQ(lambda e, g=TAIL_DONE: e.wait_ge(g[0], g[1]))
        w2slot_gate = [None]
        for tc in range(4):
            bsx2, bsq2 = banks[6], banks[7]
            bk_acquire(bsx2)
            bk_acquire(bsq2)
            sq_gate = None
            for cb in range(NB):
                so = HB + tc * 512 + (0 if cb < 4 else -1)
                if sq_gate is not None:
                    AQ(lambda e, g=sq_gate: e.wait_ge(g[0], g[1]))
                emit(AQ, lambda e, cb=cb, so=so: e.activation(
                    khat[:, 0:512], yb(cb)[:, so:so + 512], AF.Square, bias=0.0,
                    scale=1.0), inc(s_act))
                wait(TQ, s_act)
                emit(TQ, lambda e, cb=cb, so=so, bsx2=bsx2: e.matmul(
                    bsx2.ap[0:1, 0:512], ones128[:, 0:1], yb(cb)[:, so:so + 512],
                    start=(cb == 0), stop=(cb == NB - 1)),
                    bsx2.publish() if cb == NB - 1 else None)
                emit(TQ, lambda e, cb=cb, bsq2=bsq2: e.matmul(
                    bsq2.ap[0:1, 0:512], ones128[:, 1:2], khat[:, 0:512],
                    start=(cb == 0), stop=(cb == NB - 1)),
                    bsq2.publish() if cb == NB - 1 else inc(s_pe))
                sq_gate = ((bsq2.rd.h, bsq2.rd.n) if cb == NB - 1
                           else (s_pe.h, s_pe.n))
            bsx2.cwait(VQ)
            bsq2.cwait(VQ)
            emit(VQ, lambda e, bsx2=bsx2: e.tensor_scalar(
                fscr[0:1, 0:512], bsx2.ap[0:1, 0:512], 1.0 / 1024.0, None, ALU.mult),
                inc(s_dve))
            bk_gate_now(bsx2, s_dve)
            emit(VQ, lambda e, bsq2=bsq2: e.tensor_scalar(
                stage[0:1, 0:512], bsq2.ap[0:1, 0:512], 1.0 / 1024.0, None,
                ALU.mult), inc(s_dve))
            bk_gate_now(bsq2, s_dve)
            emit(VQ, lambda e: e.tensor_tensor(
                khat[0:1, 0:512], fscr[0:1, 0:512], fscr[0:1, 0:512], ALU.mult))
            emit(VQ, lambda e: e.tensor_tensor(
                stage[0:1, 0:512], stage[0:1, 0:512], khat[0:1, 0:512],
                ALU.subtract), inc(s_dve))
            wait(AQ, s_dve)
            emit(AQ, lambda e: e.activation(
                stage[0:1, 0:512], stage[0:1, 0:512], AF.Ln, bias=1e-5,
                scale=1.0))
            emit(AQ, lambda e, tc=tc: e.activation(
                rowsA[0:1, tc * 512:(tc + 1) * 512], stage[0:1, 0:512], AF.Exp,
                bias=0.0, scale=-0.5), inc(s_act))
            wait(VQ, s_act)
            emit(VQ, lambda e, tc=tc: e.scalar_tensor_tensor(
                rowsB[0:1, tc * 512:(tc + 1) * 512], fscr[0:1, 0:512], -1.0,
                rowsA[0:1, tc * 512:(tc + 1) * 512], ALU.mult, ALU.mult),
                inc(s_dve))
            ROWS_DONE = (s_dve.h, s_dve.n)
            TQ(lambda e, g=ROWS_DONE: e.wait_ge(g[0], g[1]))
            for cb in range(NB):
                so = HB + tc * 512 + (0 if cb < 4 else -1)
                bk2, bk3 = banks[2], banks[3]
                bk_acquire(bk2)
                bk_acquire(bk3)
                emit(TQ, lambda e, cb=cb, tc=tc, bk2=bk2: e.matmul(
                    bk2.ap[:, :], glbt[0:1, cb * 128:(cb + 1) * 128],
                    rowsA[0:1, tc * 512:(tc + 1) * 512], start=True, stop=True),
                    bk2.publish())
                emit(TQ, lambda e, cb=cb, tc=tc, bk3=bk3: e.matmul(
                    bk3.ap[:, :], glbt[0:2, cb * 128:(cb + 1) * 128],
                    rowsB[0:2, tc * 512:(tc + 1) * 512], start=True, stop=True),
                    bk3.publish())
                bk2.cwait(VQ)
                bk3.cwait(VQ)
                emit(VQ, lambda e, cb=cb, so=so, bk2=bk2: e.tensor_tensor(
                    khat[:, 512:1024], yb(cb)[:, so:so + 512], bk2.ap[:, :],
                    ALU.mult), inc(s_dve))
                bk_gate_now(bk2, s_dve)
                emit(VQ, lambda e, cb=cb, tc=tc, bk3=bk3: e.tensor_tensor(
                    s1b(cb)[:, tc * 512:(tc + 1) * 512], khat[:, 512:1024],
                    bk3.ap[:, :], ALU.add), inc(s_dve))
                bk_gate_now(bk3, s_dve)
            XN_DONE = (s_dve.h, s_dve.n)
            TQ(lambda e, g=XN_DONE: e.wait_ge(g[0], g[1]))
            for fb in range(NFB1):
                slot = w1ring[:, (fb % 2) * 2048:((fb % 2) + 1) * 2048]
                g = getattr(inner_layer, f"_w1g{fb % 2}", None)
                if g is not None:
                    GQ(lambda e, g=g: e.wait_ge(g[0], g[1]))
                emit(GQ, lambda e, l=l, fb=fb, slot=slot: e.dma_start(
                    out=slot[:, :].rearrange("p (c f) -> p c f", f=256),
                    in_=d_w1[l, fb].rearrange("c p f -> p c f")),
                    inc(s_ld, 16))
                wait(TQ, s_ld)
                bka, bkg = banks[0], banks[1]
                bk_acquire(bka)
                bk_acquire(bkg)
                for cb in range(NB):
                    emit(TQ, lambda e, cb=cb, tc=tc, slot=slot, bka=bka: e.matmul(
                        bka.ap[:, :], slot[:, cb * 256:cb * 256 + 128],
                        s1b(cb)[:, tc * 512:(tc + 1) * 512],
                        start=(cb == 0), stop=(cb == NB - 1)),
                        bka.publish() if cb == NB - 1 else None)
                for cb in range(NB):
                    emit(TQ, lambda e, cb=cb, tc=tc, slot=slot, bkg=bkg: e.matmul(
                        bkg.ap[:, :], slot[:, cb * 256 + 128:(cb + 1) * 256],
                        s1b(cb)[:, tc * 512:(tc + 1) * 512],
                        start=(cb == 0), stop=(cb == NB - 1)),
                        bkg.publish() if cb == NB - 1 else None)
                setattr(inner_layer, f"_w1g{fb % 2}", (bkg.rd.h, bkg.rd.n))
                bkg.cwait(AQ)
                emit(AQ, lambda e, fb=fb, bkg=bkg: e.activation(
                    khat[:, 1024:1536], bkg.ap[:, :], AF.Gelu,
                    bias=b1s[:, NFB1 + fb:NFB1 + fb + 1], scale=1.0), inc(s_act))
                bk_gate_now(bkg, s_act)
                bka.cwait(VQ)
                wait(VQ, s_act)
                emit(VQ, lambda e, fb=fb, bka=bka: e.scalar_tensor_tensor(
                    pool[:, fb * 512:(fb + 1) * 512], bka.ap[:, :],
                    b1s[:, fb:fb + 1], khat[:, 1024:1536], ALU.add, ALU.mult),
                    inc(s_dve))
                bk_gate_now(bka, s_dve)
            U_DONE = (s_dve.h, s_dve.n)
            TQ(lambda e, g=U_DONE: e.wait_ge(g[0], g[1]))
            for ob in range(NB):
                if w2slot_gate[0] is not None:
                    GQ(lambda e, g=w2slot_gate[0]: e.wait_ge(g[0], g[1]))
                emit(GQ, lambda e, l=l, ob=ob: e.dma_start(
                    out=w2ring[:, :].rearrange("p (c f) -> p c f", f=128),
                    in_=d_w2[l, ob].rearrange("c p f -> p c f")),
                    inc(s_ld, 16))
                wait(TQ, s_ld)
                bk = banks[4 + (ob % 2)]
                bk_acquire(bk)
                for cb in range(NFB1):
                    emit(TQ, lambda e, cb=cb, bk=bk: e.matmul(
                        bk.ap[:, :], w2ring[:, cb * 128:(cb + 1) * 128],
                        pool[:, cb * 512:(cb + 1) * 512],
                        start=(cb == 0), stop=(cb == NFB1 - 1)),
                        bk.publish() if cb == NFB1 - 1 else None)
                w2slot_gate[0] = (bk.rd.h, bk.rd.n)
                bk.cwait(VQ)
                emit(VQ, lambda e, ob=ob, tc=tc, bk=bk: e.scalar_tensor_tensor(
                    yb(ob)[:, HB + tc * 512:HB + (tc + 1) * 512], bk.ap[:, :],
                    b2s[:, ob:ob + 1],
                    yb(ob)[:, HB + tc * 512:HB + (tc + 1) * 512], ALU.add, ALU.add),
                    inc(s_dve))
                bk_gate_now(bk, s_dve)
        ygate[0] = (s_dve.h, s_dve.n)

        if l < L - 1:
            hstg = scr1[:, 0:NB * HB]
            VQ(lambda e, g=ygate[0]: e.wait_ge(g[0], g[1]))
            emit(VQ, lambda e, hstg=hstg: e.tensor_copy(
                hstg[:, :].rearrange("p (b t) -> p b t", t=HB),
                y[:, :].rearrange("p (b t) -> p b t", t=TTW)[:, :, T:TTW]),
                inc(s_dve))
            wait(GQ, s_dve)
            emit(GQ, lambda e, hstg=hstg: e.dma_start(
                out=d_hbin.tensor.ap(), in_=hstg[:, :]), inc(s_hd, 16))
            wait(GQ, s_hd)
            emit(GQ, lambda e: e.collective_compute(
                "AllGather", ALU.bypass, replica_groups=[list(range(NCORES))],
                ins=[d_hbin.tensor.ap().opt()], outs=[d_hbout.tensor.ap().opt()]),
                inc(s_cc))
            wait(GQ, s_cc)
            emit(GQ, lambda e, hstg=hstg: e.indirect_dma_start(
                out=hstg[:, :], out_offset=None,
                in_=d_hbout.tensor.ap(),
                in_offset=bass.IndirectOffsetOnAxis(ap=hidx_s[:, :], axis=0)),
                inc(s_hd, 16))
            wait(VQ, s_hd)
            emit(VQ, lambda e, hstg=hstg: e.tensor_copy(
                y[:, :].rearrange("p (b t) -> p b t", t=TTW)[:, :, 0:HB],
                hstg[:, :].rearrange("p (b t) -> p b t", t=HB)), inc(s_dve))
            emit(VQ, lambda e: e.tensor_scalar(
                y[:, :].rearrange("p (b t) -> p b t", t=TTW)[:, :, HB - 1:HB],
                y[:, :].rearrange("p (b t) -> p b t", t=TTW)[:, :, HB - 1:HB],
                hsc[:, 0:1], None, ALU.mult), inc(s_dve))
            ygate[0] = (s_dve.h, s_dve.n)

    def dump_stream_y(nm):
        if nm not in dump_outs:
            return
        dd = dump_outs[nm]
        GQ(lambda e, g=ygate[0]: e.wait_ge(g[0], g[1]))
        for b in range(NB):
            emit(GQ, lambda e, b=b, dd=dd: e.dma_start(
                out=dd[b * 128:(b + 1) * 128, :], in_=yb(b)[:, HB:]), inc(s_hd, 16))
        wait(GQ, s_hd)
        for Qx in (TQ, VQ, AQ, SQ):
            wait(Qx, s_hd)

    dump_stream_y("xn")

    if do_perf:
        performer()
        if "s" in dump_outs:
            dd = dump_outs["s"]
            wait(GQ, s_st)
            for b in range(NB):
                emit(GQ, lambda e, b=b, dd=dd: e.dma_start(
                    out=dd[b * 128:(b + 1) * 128, :],
                    in_=d_ssp[b * 128:(b + 1) * 128, :]), inc(s_hd, 16))
            wait(GQ, s_hd)
            for Qx in (TQ, VQ, AQ, SQ):
                wait(Qx, s_hd)
        for i in (0, 1):
            g = getattr(performer, f"_wg{i}", None)
            if g is not None:
                setattr(inner_layer, f"_wg{i}", g)

    for l in range(L):
        inner_layer(l, first=(l == 0))
        ydump = f"y{l}"
        dump_stream_y(ydump)

    AQ(lambda e, g=ygate[0]: e.wait_ge(g[0], g[1]))
    gslot = pool[:, 0:T]
    for b in range(NB):
        emit(AQ, lambda e, b=b: e.activation(
            gslot[:, :], yb(b)[:, HB:], AF.Silu, bias=0.0, scale=1.0), inc(s_act))
        GDONE = (s_act.h, s_act.n)
        for tc in range(4):
            wait(SQ, s_dve)
            emit(SQ, lambda e, b=b, tc=tc: e.dma_start(
                out=stage[:, 0:512],
                in_=d_xT[b * 128:(b + 1) * 128, HB + tc * 512:HB + (tc + 1) * 512]),
                inc(s_ld, 16))
            emit(SQ, lambda e, b=b, tc=tc: e.dma_start(
                out=stage[:, 512:1024],
                in_=d_ssp[b * 128:(b + 1) * 128, tc * 512:(tc + 1) * 512]),
                inc(s_ld, 16))
            wait(VQ, s_ld)
            VQ(lambda e, g=GDONE: e.wait_ge(g[0], g[1]))
            emit(VQ, lambda e, b=b, tc=tc: e.tensor_tensor(
                stage[:, 512:1024], stage[:, 512:1024],
                gslot[:, tc * 512:(tc + 1) * 512], ALU.mult))
            emit(VQ, lambda e: e.tensor_tensor(
                stage[:, 0:512], stage[:, 0:512], stage[:, 512:1024], ALU.add),
                inc(s_dve))
            wait(SQ, s_dve)
            emit(SQ, lambda e, b=b, tc=tc: e.dma_start(
                out=d_out[b * 128:(b + 1) * 128, tc * 512:(tc + 1) * 512],
                in_=stage[:, 0:512]), inc(s_st, 16))
        AQ(lambda e, h=s_st.h, v=s_st.n: e.wait_ge(h, v))
    wait(SQ, s_st, s_st.n)
    with nc.Block() as block:
        @block.tensor
        def _(t):
            for f in OPS["tensor"]:
                f(t)

        @block.scalar
        def _(s):
            for f in OPS["scalar"]:
                f(s)

        @block.vector
        def _(v):
            for f in OPS["vector"]:
                f(v)

        @block.gpsimd
        def _(g):
            for f in OPS["gpsimd"]:
                f(g)

        @block.sync
        def _(s):
            for f in OPS["sync"]:
                f(s)

    return nc, dump_outs


def kernel(**inputs):
    maps = host_prep(inputs)
    nc, _ = build()
    res = run_bass_kernel_spmd(nc, maps, core_ids=list(range(NCORES)))
    parts = [np.asarray(r["out"], np.float32).T for r in res.results]
    return np.concatenate(parts, axis=0).reshape(1, NCORES * T, D)


# version 05 (final)
# speedup vs baseline: 1.0752x; 1.0752x over previous
import numpy as np
import ml_dtypes
from contextlib import ExitStack
import concourse.bass as bass
import concourse.mybir as mybir
from concourse.bass_utils import run_bass_kernel_spmd

F32 = mybir.dt.float32
BF16 = mybir.dt.bfloat16
U32 = mybir.dt.uint32
AF = mybir.ActivationFunctionType
ALU = mybir.AluOpType
AX = mybir.AxisListType

NCORES = 8
T = 2048
HB = 258
TTW = HB + T
KT = 2304
NKT = 18
D = 1024
NB = 8
H = 16
HD = 64
R = 256
DI = 2816
NFB1 = 22
QKS = 8.0
DEPTH = 6
BF = ml_dtypes.bfloat16

CONSTVALS = [8.0, 0.5, -0.5, -1.0, 2.0, 1e-17, 1.0 / D, 1e-5, 1e-6, -8.0, 1.0,
             1.0 / 1024.0]


def _f32(x):
    return np.ascontiguousarray(x, dtype=np.float32)


def host_prep(inputs, nlayers=DEPTH):
    x = _f32(inputs["L0_attn_x"][0])
    S = x.shape[0]
    qs_all = _f32(inputs["L0_ffn_q_scale"])
    ks_all = _f32(inputs["L0_ffn_k_scale"])
    Wqkv = _f32(inputs["L0_ffn_Wqkv"])
    Wo = _f32(inputs["L0_ffn_Wo"])
    lg = _f32(inputs["L0_ffn_ln_g"])
    lb = _f32(inputs["L0_ffn_ln_b"])
    W1 = _f32(inputs["L0_ffn_W1"])
    b1 = _f32(inputs["L0_ffn_b1"])
    W2 = _f32(inputs["L0_ffn_W2"])
    b2 = _f32(inputs["L0_ffn_b2"])
    L = nlayers

    sh = {}
    sh["wqT"] = _f32(inputs["L0_attn_wq"].T)
    sh["wkT"] = _f32(inputs["L0_attn_wk"].T)
    sh["wvT"] = _f32(inputs["L0_attn_wv"].T)
    sh["woT"] = _f32(inputs["L0_attn_wo"].T)
    rf = _f32(inputs["L0_attn_random_features"])
    rfd = np.zeros((128, H * R), np.float32)
    rfa = np.zeros((65, H * R), np.float32)
    for h in range(H):
        rfd[0:64, h * R:(h + 1) * R] = rf[h]
        rfd[64:128, h * R:(h + 1) * R] = rf[h]
        rfa[0:64, h * R:(h + 1) * R] = rf[h]
    rfa[64, :] = -1.0
    sh["rfd"], sh["rfa"] = rfd, rfa
    for nm, key in (("bq", "L0_attn_bq"), ("bk", "L0_attn_bk"), ("bo", "L0_attn_bo")):
        sh[nm] = np.ascontiguousarray(_f32(inputs[key]).reshape(NB, 128).T)
    sh["bvrow"] = _f32(inputs["L0_attn_bv"]).reshape(1, D)

    sh["wqkv_s"] = np.ascontiguousarray(
        Wqkv[:L].reshape(L, NB, 128, 24, 128).transpose(0, 3, 1, 2, 4))
    sh["wo_s"] = np.ascontiguousarray(
        Wo[:L].reshape(L, NB, 128, NB, 128).transpose(0, 3, 1, 2, 4))
    W1p = np.zeros((L, D, 2 * DI), np.float32)
    W1p[:, :, 0:2730] = W1[:L, :, 0:2730]
    W1p[:, :, DI:DI + 2730] = W1[:L, :, 2730:5460]
    w1v = W1p.reshape(L, NB, 128, 2, NFB1, 128)
    sh["w1_s"] = np.ascontiguousarray(
        w1v.transpose(0, 4, 1, 2, 3, 5).reshape(L, NFB1, NB, 128, 256))
    W2p = np.zeros((L, DI, D), np.float32)
    W2p[:, 0:2730, :] = W2[:L]
    sh["w2_s"] = np.ascontiguousarray(
        W2p.reshape(L, NFB1, 128, NB, 128).transpose(0, 3, 1, 2, 4))
    b1p = np.zeros((L, 2 * DI), np.float32)
    b1p[:, 0:2730] = b1[:L, 0:2730]
    b1p[:, DI:DI + 2730] = b1[:L, 2730:5460]
    sh["b1c"] = np.ascontiguousarray(b1p.reshape(L, 2 * NFB1, 128).transpose(0, 2, 1))
    sh["b2c"] = np.ascontiguousarray(b2[:L].reshape(L, NB, 128).transpose(0, 2, 1))
    glb = np.zeros((L, NB, 2, 128), np.float32)
    for l in range(L):
        glb[l, :, 0, :] = lg[l].reshape(NB, 128)
        glb[l, :, 1, :] = lb[l].reshape(NB, 128)
    sh["glb"] = glb
    ind = np.zeros((L, 2, NB, H, 128), np.float32)
    for l in range(L):
        for b in range(NB):
            ind[l, 0, b, 2 * b, 0:64] = qs_all[l]
            ind[l, 0, b, 2 * b + 1, 64:128] = qs_all[l]
            ind[l, 1, b, 2 * b, 0:64] = ks_all[l]
            ind[l, 1, b, 2 * b + 1, 64:128] = ks_all[l]
    sh["indqk"] = ind
    ind16 = np.zeros((NB, 128, H), np.float32)
    for b in range(NB):
        ind16[b, 0:64, 2 * b] = 1.0
        ind16[b, 64:128, 2 * b + 1] = 1.0
    sh["ind16"] = np.ascontiguousarray(ind16.transpose(1, 0, 2).reshape(128, NB * H))
    indh = np.zeros((128, 4), np.float32)
    indh[0:64, 0] = -0.5
    indh[64:128, 1] = -0.5
    indh[0:64, 2] = 0.5
    indh[64:128, 3] = 0.5
    sh["indh"] = indh
    srow = np.zeros((1, max(L, 1) * 128), np.float32)
    for l in range(L):
        srow[0, l * 128:l * 128 + 64] = qs_all[l]
        srow[0, l * 128 + 64:l * 128 + 128] = ks_all[l]
    sh["srow"] = srow
    sh["consts"] = np.tile(np.array(CONSTVALS, np.float32), (128, 1))
    sh["eye"] = np.eye(128, dtype=np.float32)

    maps = []
    for ci in range(NCORES):
        m = dict(sh)
        st = ci * T
        lo = st - HB
        xs = np.zeros((TTW, D), np.float32)
        a = max(lo, 0)
        xs[a - lo:TTW] = x[a:st + T]
        m["xT"] = np.ascontiguousarray(xs.T)
        tok = np.arange(st - 256, st + T, dtype=np.float64)
        half = np.arange(0, HD, 2, dtype=np.float64)
        inv_freq = 1.0 / 10000.0 ** (half / HD)
        fr = tok[None, :] * inv_freq[:, None]
        fr = np.concatenate([fr, fr], 0)
        base = (half + 0.4 * HD) / (1.4 * HD)
        power = (tok - S // 2) / 128.0
        sc = base[:, None] ** power[None, :]
        sc = np.concatenate([sc, sc], 0)
        cos, sin = np.cos(fr), np.sin(fr)
        sgn = np.ones((HD, 1))
        sgn[0:32] = -1.0
        aq = (cos * sc).astype(np.float32)
        bq = (sgn * sin * sc).astype(np.float32)
        ak = (cos / sc).astype(np.float32)
        bk_ = (sgn * sin / sc).astype(np.float32)
        m["rotAq"] = np.concatenate([aq[:, 256:], aq[:, 256:]], 0)
        m["rotBq"] = np.concatenate([bq[:, 256:], bq[:, 256:]], 0)
        m["rotAk"] = np.concatenate([ak, ak], 0)
        m["rotBk"] = np.concatenate([bk_, bk_], 0)
        kr = np.arange(128)[:, None]
        mm = np.arange(384)[None, :]
        interior = ((kr <= mm) & (mm <= kr + 256)).astype(np.float32)
        mk = np.zeros((128, 768), np.float32)
        mk[:, 0:384] = interior
        if ci > 0:
            mk[:, 384:512] = interior[:, 256:384]
            mk[:, 512:768] = interior[:, 128:384]
        m["masks"] = mk
        m["haloscale"] = np.full((128, 1), 0.0 if ci == 0 else 1.0, np.float32)
        left = (ci - 1) % NCORES
        m["hidx"] = (np.arange(128, dtype=np.uint32) + 128 * left).reshape(128, 1)
        maps.append(m)
    return maps


def build(nlayers=DEPTH, do_perf=True, dumps=()):
    nc = bass.Bass("TRN2", target_bir_lowering=False)
    es = ExitStack()
    L = nlayers

    def din(name, shape, dt=F32):
        return nc.dram_tensor(name, list(shape), dt, kind="ExternalInput").ap()

    def dout(name, shape, dt=F32):
        return nc.dram_tensor(name, list(shape), dt, kind="ExternalOutput").ap()

    def sbuf(name, shape, dt=BF16):
        return es.enter_context(nc.sbuf_tensor(name, list(shape), dt))

    def psm(name, shape, dt=F32):
        return es.enter_context(nc.psum_tensor(name, list(shape), dt))

    class Ctr:
        __slots__ = ("h", "n")

        def __init__(self, name):
            self.h = es.enter_context(nc.semaphore(name))
            self.n = 0

    d_xT = din("xT", [D, TTW])
    d_rotAq = din("rotAq", [128, T]); d_rotBq = din("rotBq", [128, T])
    d_rotAk = din("rotAk", [128, KT]); d_rotBk = din("rotBk", [128, KT])
    d_masks = din("masks", [128, 768])
    d_hs = din("haloscale", [128, 1])
    d_hidx = din("hidx", [128, 1], U32)
    d_consts = din("consts", [128, len(CONSTVALS)])
    d_wqT = din("wqT", [D, D]); d_wkT = din("wkT", [D, D])
    d_wvT = din("wvT", [D, D]); d_woT = din("woT", [D, D])
    d_rfd = din("rfd", [128, H * R]); d_rfa = din("rfa", [65, H * R])
    d_bq = din("bq", [128, NB]); d_bk = din("bk", [128, NB]); d_bo = din("bo", [128, NB])
    d_bvrow = din("bvrow", [1, D])
    d_eye = din("eye", [128, 128])
    d_wqkv = din("wqkv_s", [L, 24, NB, 128, 128])
    d_wo = din("wo_s", [L, NB, NB, 128, 128])
    d_w1 = din("w1_s", [L, NFB1, NB, 128, 256])
    d_w2 = din("w2_s", [L, NB, NFB1, 128, 128])
    d_b1c = din("b1c", [L, 128, 2 * NFB1])
    d_b2c = din("b2c", [L, 128, NB])
    d_glb = din("glb", [L, NB, 2, 128])
    d_ind = din("indqk", [L, 2, NB, H, 128])
    d_ind16 = din("ind16", [128, NB * H])
    d_indh = din("indh", [128, 4])
    d_srow = din("srow", [1, max(L, 1) * 128])
    d_out = dout("out", [D, T])
    dump_outs = {nm: dout("dump_" + nm, [D, T]) for nm in dumps}
    d_ssp = nc.dram_tensor("s_sp", [D, T], F32).ap()
    d_hbin = nc.dram_tensor("hb_in", [128, NB * HB], BF16).ap()
    d_hbout = nc.dram_tensor("hb_out", [NCORES * 128, NB * HB], BF16).ap()
    d_tbin = nc.dram_tensor("tb_in", [128, NB], BF16).ap()
    d_tbout = nc.dram_tensor("tb_out", [NCORES * 128, NB], BF16).ap()
    d_kvin = nc.dram_tensor("kv_in", [128, H * 130], F32).ap()
    d_kvout = nc.dram_tensor("kv_out", [128, H * 130], F32).ap()

    y = sbuf("y", [128, NB * TTW])

    def yb(b):
        return y[:, b * TTW:(b + 1) * TTW]

    v_tok = sbuf("v_tok", [128, NKT * H * 65])

    def vt(tt):
        return v_tok[:, tt * H * 65:(tt + 1) * H * 65]

    scr1 = sbuf("scr1", [128, NB * T])

    def s1b(b):
        return scr1[:, b * T:(b + 1) * T]

    pool = sbuf("pool", [128, NFB1 * 512])
    scr2 = pool
    qhat = sbuf("qhat", [128, T])
    khat = sbuf("khat", [128, KT])
    qfeat = pool[:, 0:KT]
    qsq = pool[:, KT:2 * KT]
    rott0 = pool[:, 2 * KT:3 * KT]
    rott1 = pool[:, 3 * KT:4 * KT]
    tabAq = sbuf("tabAq", [128, T]); tabBq = sbuf("tabBq", [128, T])
    tabAk = sbuf("tabAk", [128, KT]); tabBk = sbuf("tabBk", [128, KT])
    masks_s = sbuf("masks_s", [128, 768])
    Etile = None
    Eraw = None
    rnorm = sbuf("rnorm", [16, 512])
    rowsA = sbuf("rowsA", [1, TTW])
    rowsB = sbuf("rowsB", [2, TTW])
    rrow = sbuf("rrow", [1, 512])
    consts = sbuf("consts_s", [128, len(CONSTVALS)], F32)
    cbias = sbuf("cbias", [128, 2], F32)
    hsc = sbuf("hsc", [128, 1], F32)
    hidx_s = sbuf("hidx_s", [128, 1], U32)
    glbt = sbuf("glbt", [2, NB * 128])
    indq = sbuf("indq", [16, NB * 128])
    indk = sbuf("indk", [16, NB * 128])
    ind16s = sbuf("ind16s", [128, NB * H])
    eyef = sbuf("eyef", [128, 128], F32)
    trow = sbuf("trow", [16, 128])
    indhs = sbuf("indhs", [128, 4])
    b1s = sbuf("b1s", [128, 2 * NFB1], F32)
    b2s = sbuf("b2s", [128, NB], F32)
    bqs = sbuf("bqs", [128, NB], F32)
    bks = sbuf("bks", [128, NB], F32)
    bos = sbuf("bos", [128, NB], F32)
    srow_s = sbuf("srow_s", [1, 128])
    rfds = sbuf("rfds", [128, H * R])
    rfas = None
    onesr = sbuf("onesr", [1, 128])
    ones128 = sbuf("ones128", [128, 2])
    negones = sbuf("negones", [1, 128])
    n2tok = sbuf("n2tok", [128, NB * 32], F32)
    tcol = sbuf("tcol", [128, 2 * NB])
    wrow = qhat[0:1, 0:T]
    bvr = rowsB[0:1, 0:D]
    Etile = ([rott1[:, 768 + i * 384:768 + (i + 1) * 384] for i in range(4)]
             + [rott0[:, 512 + i * 384:512 + (i + 1) * 384] for i in range(2)])
    Eraw = [rott1[:, 0:384], rott1[:, 384:768]]
    qp0 = pool[:, 2 * KT:2 * KT + T]
    qp1 = pool[:, 3 * KT:3 * KT + T]
    kvr = sbuf("kvr", [128, 2 * 65])
    kpt = [sbuf(f"kpt{i}", [128, R]) for i in range(2)]
    wring = sbuf("wring", [128, 2 * 1024])
    w1ring = sbuf("w1ring", [128, 2 * 2048])
    w2ring = sbuf("w2ring", [128, 2816])
    stage = sbuf("stage", [128, 2 * 512], F32)
    fscr = sbuf("fscr", [16, 512], F32)
    lnb16 = fscr
    rcpb = sbuf("rcpb", [65, 512])
    rcpf = stage[64:65, 512:1024]

    for j, vv in enumerate(CONSTVALS):
        nc.const_aps.register(F32, vv, consts[:, j:j + 1])

    OPS = {k: [] for k in ("tensor", "scalar", "vector", "gpsimd", "sync")}
    TQ, AQ, VQ, GQ, SQ = (OPS[k].append for k in ("tensor", "scalar", "vector",
                                                  "gpsimd", "sync"))

    def emit(Q, fn, incs=None):
        if incs is None:
            Q(fn)
            return
        ii = incs if isinstance(incs, list) else [incs]

        def g(e, fn=fn, ii=ii):
            inst = fn(e)
            for (hh, aa) in ii:
                inst = inst.then_inc(hh, aa)
            return inst
        Q(g)

    def inc(ctr, amt=1):
        ctr.n += amt
        return (ctr.h, amt)

    def wait(Q, ctr, v=None):
        vv = ctr.n if v is None else v
        Q(lambda e, h=ctr.h, vv=vv: e.wait_ge(h, vv))

    s_ld = Ctr("s_ld"); s_w = Ctr("s_w"); s_pe = Ctr("s_pe")
    s_act = Ctr("s_act"); s_dve = Ctr("s_dve"); s_gp = Ctr("s_gp")
    s_cc = Ctr("s_cc"); s_hd = Ctr("s_hd"); s_st = Ctr("s_st")
    s_wf = Ctr("s_wf")

    PS = [psm(f"p{i}", [128, 512]) for i in range(8)]
    s_pr = [Ctr(f"s_pr{i}") for i in range(8)]
    s_pf = [Ctr(f"s_pf{i}") for i in range(8)]

    class Bank:
        def __init__(self, i):
            self.i = i
            self.ap = PS[i]
            self.rd = s_pr[i]
            self.fr = s_pf[i]

        def acquire(self):
            if self.fr.n:
                wait(TQ, self.fr)

        def publish(self):
            return inc(self.rd)

        def cwait(self, Q):
            wait(Q, self.rd)

        def release(self):
            return inc(self.fr)

    banks = [Bank(i) for i in range(8)]

    for dst, src in ((consts, d_consts), (hsc, d_hs), (hidx_s, d_hidx),
                     (bqs, d_bq), (bks, d_bk), (bos, d_bo)):
        emit(SQ, lambda e, dd=dst, ss=src: e.dma_start(out=dd[:, :], in_=ss),
             inc(s_ld, 16))
    for dst, src in ((masks_s, d_masks), (tabAq, d_rotAq), (tabBq, d_rotBq), (tabAk, d_rotAk), (tabBk, d_rotBk),
                     (rfds, d_rfd), (ind16s, d_ind16),
                     (indhs, d_indh), (eyef, d_eye)):
        emit(GQ, lambda e, dd=dst, ss=src: e.dma_start(out=dd[:, :], in_=ss),
             inc(s_ld, 16))
    LD0 = s_ld.n
    emit(VQ, lambda e: e.memset(y[:, :], 0.0), inc(s_dve))
    emit(VQ, lambda e: e.memset(rowsA[:, :], 1.0), inc(s_dve))
    emit(VQ, lambda e: e.memset(rowsB[:, :], 1.0), inc(s_dve))
    emit(VQ, lambda e: e.memset(onesr[:, :], 1.0), inc(s_dve))
    emit(VQ, lambda e: e.memset(ones128[:, :], 1.0), inc(s_dve))
    emit(VQ, lambda e: e.memset(negones[:, :], -1.0), inc(s_dve))
    emit(VQ, lambda e: e.memset(v_tok[:, :], 0.0), inc(s_dve))
    emit(VQ, lambda e: e.memset(
        v_tok[:, :].rearrange("p (tt h d) -> p tt h d", h=H, d=65)[:, :, :, 64:65],
        1.0), inc(s_dve))
    MEMS0 = s_dve.n

    wait(GQ, s_ld, LD0)
    wait(GQ, s_dve, MEMS0)
    for b in range(NB):
        emit(GQ, lambda e, b=b: e.dma_start(
            out=yb(b)[:, :], in_=d_xT[b * 128:(b + 1) * 128, :]), inc(s_ld, 16))
    LDX = s_ld.n
    wait(TQ, s_ld, LDX)
    wait(AQ, s_ld, LDX)
    wait(TQ, s_dve, MEMS0)

    ln_chunks = [(i * 512, min(512, TTW - i * 512)) for i in range((TTW + 511) // 512)]
    bsx, bsq = banks[6], banks[7]
    qsq_gate = (s_pe, 0)
    for (off, w) in ln_chunks:
        bsx.acquire()
        bsq.acquire()
        for b in range(NB):
            wait(AQ, qsq_gate[0], qsq_gate[1])
            emit(AQ, lambda e, b=b, off=off, w=w: e.activation(
                qsq[:, 0:w], yb(b)[:, off:off + w], AF.Square, bias=0.0, scale=1.0),
                inc(s_act))
            wait(TQ, s_act)
            emit(TQ, lambda e, b=b, off=off, w=w: e.matmul(
                bsx.ap[0:1, 0:w], ones128[:, 0:1], yb(b)[:, off:off + w],
                start=(b == 0), stop=(b == NB - 1)),
                [bsx.publish()] if b == NB - 1 else None)
            if b == NB - 1:
                emit(TQ, lambda e, w=w, b=b: e.matmul(
                    bsq.ap[0:1, 0:w], ones128[:, 1:2], qsq[:, 0:w],
                    start=(b == 0), stop=True), bsq.publish())
                qsq_gate = (bsq.rd, bsq.rd.n)
            else:
                emit(TQ, lambda e, w=w, b=b: e.matmul(
                    bsq.ap[0:1, 0:w], ones128[:, 1:2], qsq[:, 0:w],
                    start=(b == 0), stop=False), inc(s_pe))
                qsq_gate = (s_pe, s_pe.n)
        bsx.cwait(VQ)
        bsq.cwait(VQ)
        emit(VQ, lambda e, w=w: e.tensor_scalar(
            fscr[0:1, 0:w], bsx.ap[0:1, 0:w], 1.0 / 1024.0, None, ALU.mult),
            bsx.release())
        emit(VQ, lambda e, w=w: e.tensor_scalar(
            stage[0:1, 0:w], bsq.ap[0:1, 0:w], 1.0 / 1024.0, None, ALU.mult),
            bsq.release())
        emit(VQ, lambda e, w=w: e.tensor_tensor(
            qsq[0:1, 0:w], fscr[0:1, 0:w], fscr[0:1, 0:w], ALU.mult))
        emit(VQ, lambda e, w=w: e.tensor_tensor(
            stage[0:1, 0:w], stage[0:1, 0:w], qsq[0:1, 0:w],
            ALU.subtract), inc(s_dve))
        wait(AQ, s_dve)
        emit(AQ, lambda e, w=w: e.activation(
            stage[0:1, 0:w], stage[0:1, 0:w], AF.Ln, bias=1e-5,
            scale=1.0))
        emit(AQ, lambda e, off=off, w=w: e.activation(
            rowsA[0:1, off:off + w], stage[0:1, 0:w], AF.Exp, bias=0.0,
            scale=-0.5), inc(s_act))
        wait(VQ, s_act)
        emit(VQ, lambda e, off=off, w=w: e.scalar_tensor_tensor(
            rowsB[0:1, off:off + w], fscr[0:1, 0:w], -1.0, rowsA[0:1, off:off + w],
            ALU.mult, ALU.mult), inc(s_dve))
    LNROWS_DVE = s_dve.n

    bpa, bpb = banks[4], banks[5]
    wait(TQ, s_dve, LNROWS_DVE)
    for b in range(NB):
        for (off, w) in ln_chunks:
            bpa.acquire()
            bpb.acquire()
            emit(TQ, lambda e, off=off, w=w: e.matmul(
                bpa.ap[:, 0:w], onesr[:, :], rowsA[0:1, off:off + w],
                start=True, stop=True), bpa.publish())
            emit(TQ, lambda e, off=off, w=w: e.matmul(
                bpb.ap[:, 0:w], onesr[:, :], rowsB[0:1, off:off + w],
                start=True, stop=True), bpb.publish())
            bpa.cwait(VQ)
            bpb.cwait(VQ)
            emit(VQ, lambda e, b=b, off=off, w=w: e.tensor_tensor(
                rott0[:, 0:w], yb(b)[:, off:off + w], bpa.ap[:, 0:w], ALU.mult),
                bpa.release())
            emit(VQ, lambda e, b=b, off=off, w=w: e.tensor_tensor(
                yb(b)[:, off:off + w], rott0[:, 0:w], bpb.ap[:, 0:w], ALU.add),
                bpb.release())
    LNDONE = (bpb.fr.h, bpb.fr.n)

    ygate = [(bpb.fr.h, bpb.fr.n)]

    for bk in banks:
        bk.gate = None

    def bk_acquire(bk, Q=TQ):
        if bk.gate is not None:
            Q(lambda e, g=bk.gate: e.wait_ge(g[0], g[1]))

    def bk_gate_now(bk, ctr):
        bk.gate = (ctr.h, ctr.n)

    def performer():
        afg = None
        for b in range(NB):
            slot = wring[:, (b % 2) * 1024:(b % 2 + 1) * 1024]
            g = getattr(performer, f"_wg{b % 2}", None)
            if g is not None:
                GQ(lambda e, g=g: e.wait_ge(g[0], g[1]))
            GQ(lambda e, g=ygate[0]: e.wait_ge(g[0], g[1]))
            emit(GQ, lambda e, b=b, slot=slot: e.dma_start(
                out=slot[:, :].rearrange("p (c f) -> p c f", f=128),
                in_=d_wkT[:, b * 128:(b + 1) * 128].rearrange(
                    "(c p) f -> p c f", p=128)), inc(s_ld, 16))
            wait(TQ, s_ld)
            TQ(lambda e, g=ygate[0]: e.wait_ge(g[0], g[1]))
            lastbk = None
            for ch in range(4):
                bk = banks[ch % 2]
                bk_acquire(bk)
                for cb in range(NB):
                    emit(TQ, lambda e, cb=cb, ch=ch, slot=slot, bk=bk: e.matmul(
                        bk.ap[:, :], slot[:, cb * 128:(cb + 1) * 128],
                        yb(cb)[:, HB + ch * 512:HB + (ch + 1) * 512],
                        start=(cb == 0), stop=(cb == NB - 1)),
                        bk.publish() if cb == NB - 1 else None)
                bk.cwait(VQ)
                emit(VQ, lambda e, b=b, ch=ch, bk=bk: e.tensor_scalar(
                    s1b(b)[:, ch * 512:(ch + 1) * 512], bk.ap[:, :],
                    bks[:, b:b + 1], None, ALU.add), inc(s_dve))
                bk_gate_now(bk, s_dve)
                lastbk = bk
            setattr(performer, f"_wg{b % 2}", (lastbk.rd.h, lastbk.rd.n))
            wait(AQ, s_dve)
            emit(AQ, lambda e, b=b: e.activation(
                qsq[:, 0:T], s1b(b)[:, :], AF.Square, bias=0.0, scale=1.0),
                inc(s_act))
            bk2 = banks[2]
            bk_acquire(bk2)
            wait(TQ, s_act)
            for tt in range(16):
                emit(TQ, lambda e, tt=tt, bk2=bk2: e.matmul(
                    bk2.ap[:, 2 * tt:2 * tt + 2], qsq[:, tt * 128:(tt + 1) * 128],
                    indhs[:, 0:2], start=True, stop=True),
                    bk2.publish() if tt == 15 else None)
            bk2.cwait(VQ)
            emit(VQ, lambda e, b=b, bk2=bk2: e.tensor_copy(
                n2tok[:, b * 32:(b + 1) * 32], bk2.ap[:, 0:32]), inc(s_dve))
            bk_gate_now(bk2, s_dve)
        KPROJ_DONE = (s_dve.h, s_dve.n)

        wvres = pool[:, 0:8 * 1024]
        GQ(lambda e, g=KPROJ_DONE: e.wait_ge(g[0], g[1]))
        emit(GQ, lambda e: e.dma_start(out=bvr[:, :], in_=d_bvrow), inc(s_ld, 16))
        for cb in range(NB):
            emit(GQ, lambda e, cb=cb: e.dma_start(
                out=wvres[:, cb * 1024:(cb + 1) * 1024],
                in_=d_wvT[cb * 128:(cb + 1) * 128, :]), inc(s_ld, 16))
        wait(TQ, s_ld)
        for tt in range(16):
            for fc in range(2):
                bk = banks[(tt * 2 + fc) % 2]
                bk_acquire(bk)
                for cb in range(NB):
                    emit(TQ, lambda e, cb=cb, tt=tt, fc=fc, bk=bk: e.matmul(
                        bk.ap[:, :], yb(cb)[:, HB + 128 * tt:HB + 128 * tt + 128],
                        wvres[:, cb * 1024 + fc * 512:cb * 1024 + (fc + 1) * 512],
                        start=(cb == 0), stop=False))
                emit(TQ, lambda e, fc=fc, bk=bk: e.matmul(
                    bk.ap[:, :], onesr[:, :], bvr[0:1, fc * 512:(fc + 1) * 512],
                    start=False, stop=True), bk.publish())
                bk.cwait(VQ)
                emit(VQ, lambda e, tt=tt, fc=fc, bk=bk: e.tensor_copy(
                    vt(tt).rearrange("p (h d) -> p h d", d=65)[:, fc * 8:(fc + 1) * 8, 0:64],
                    bk.ap[:, :].rearrange("p (h d) -> p h d", d=64)), inc(s_dve))
                bk_gate_now(bk, s_dve)
        V_DONE = (s_dve.h, s_dve.n)

        TQ(lambda e, g=V_DONE: e.wait_ge(g[0], g[1]))
        for h in range(H):
            b, hh = h // 2, h % 2
            kb4, kb5 = banks[4], banks[5]
            bk_acquire(kb4)
            bk_acquire(kb5)
            kp_gate = [None, None]
            for tt in range(16):
                bk = banks[3 if tt % 2 == 0 else 6]
                bk_acquire(bk)
                emit(TQ, lambda e, b=b, hh=hh, h=h, tt=tt, bk=bk: e.matmul(
                    bk.ap[:, 0:R], s1b(b)[hh * 64:(hh + 1) * 64, tt * 128:(tt + 1) * 128],
                    rfds[hh * 64:(hh + 1) * 64, h * R:(h + 1) * R],
                    start=True, stop=True), bk.publish())
                bk.cwait(AQ)
                if kp_gate[tt % 2] is not None:
                    AQ(lambda e, g=kp_gate[tt % 2]: e.wait_ge(g[0], g[1]))
                emit(AQ, lambda e, b=b, hh=hh, tt=tt, bk=bk: e.activation(
                    kpt[tt % 2][:, :], bk.ap[:, 0:R], AF.Exp,
                    bias=n2tok[:, b * 32 + 2 * tt + hh:b * 32 + 2 * tt + hh + 1],
                    scale=1.0), inc(s_act))
                bk_gate_now(bk, s_act)
                wait(TQ, s_act)
                emit(TQ, lambda e, h=h, tt=tt, kb4=kb4: e.matmul(
                    kb4.ap[:, 0:65], kpt[tt % 2][:, 0:128],
                    vt(tt)[:, h * 65:(h + 1) * 65],
                    start=(tt == 0), stop=(tt == 15)),
                    kb4.publish() if tt == 15 else None)
                emit(TQ, lambda e, h=h, tt=tt, kb5=kb5: e.matmul(
                    kb5.ap[:, 0:65], kpt[tt % 2][:, 128:256],
                    vt(tt)[:, h * 65:(h + 1) * 65],
                    start=(tt == 0), stop=(tt == 15)),
                    kb5.publish() if tt == 15 else None)
                kp_gate[tt % 2] = (kb5.rd.h, kb5.rd.n)
            kb4.cwait(VQ)
            kb5.cwait(VQ)
            emit(VQ, lambda e, kb4=kb4: e.tensor_copy(
                stage[:, 0:65], kb4.ap[:, 0:65]), inc(s_dve))
            bk_gate_now(kb4, s_dve)
            emit(VQ, lambda e, kb5=kb5: e.tensor_copy(
                stage[:, 65:130], kb5.ap[:, 0:65]), inc(s_dve))
            bk_gate_now(kb5, s_dve)
            wait(SQ, s_dve)
            emit(SQ, lambda e, h=h: e.dma_start(
                out=d_kvin.tensor.ap()[:, h * 130:(h + 1) * 130],
                in_=stage[:, 0:130]), inc(s_hd, 16))
            wait(VQ, s_hd)

        wait(GQ, s_hd)
        emit(GQ, lambda e: e.collective_compute(
            "AllReduce", ALU.add, replica_groups=[list(range(NCORES))],
            ins=[d_kvin.tensor.ap().opt()], outs=[d_kvout.tensor.ap().opt()]),
            inc(s_cc))
        wait(GQ, s_cc)

        for b in range(NB):
            slot = wring[:, (b % 2) * 1024:(b % 2 + 1) * 1024]
            g = getattr(performer, f"_wg{b % 2}", None)
            if g is not None:
                GQ(lambda e, g=g: e.wait_ge(g[0], g[1]))
            emit(GQ, lambda e, b=b, slot=slot: e.dma_start(
                out=slot[:, :].rearrange("p (c f) -> p c f", f=128),
                in_=d_wqT[:, b * 128:(b + 1) * 128].rearrange(
                    "(c p) f -> p c f", p=128)), inc(s_ld, 16))
            wait(TQ, s_ld)
            lastbk = None
            for ch in range(4):
                bk = banks[ch % 2]
                bk_acquire(bk)
                for cb in range(NB):
                    emit(TQ, lambda e, cb=cb, ch=ch, slot=slot, bk=bk: e.matmul(
                        bk.ap[:, :], slot[:, cb * 128:(cb + 1) * 128],
                        yb(cb)[:, HB + ch * 512:HB + (ch + 1) * 512],
                        start=(cb == 0), stop=(cb == NB - 1)),
                        bk.publish() if cb == NB - 1 else None)
                bk.cwait(VQ)
                emit(VQ, lambda e, b=b, ch=ch, bk=bk: e.tensor_scalar(
                    qfeat[:, ch * 512:(ch + 1) * 512], bk.ap[:, :],
                    bqs[:, b:b + 1], None, ALU.add), inc(s_dve))
                bk_gate_now(bk, s_dve)
                lastbk = bk
            setattr(performer, f"_wg{b % 2}", (lastbk.rd.h, lastbk.rd.n))
            wait(AQ, s_dve)
            emit(AQ, lambda e: e.activation(
                qsq[:, 0:T], qfeat[:, 0:T], AF.Square, bias=0.0, scale=1.0),
                inc(s_act))
            bk2 = banks[2]
            bk_acquire(bk2)
            wait(TQ, s_act)
            for tt in range(16):
                emit(TQ, lambda e, tt=tt, bk2=bk2: e.matmul(
                    bk2.ap[:, 2 * tt:2 * tt + 2], qsq[:, tt * 128:(tt + 1) * 128],
                    indhs[:, 2:4], start=True, stop=True),
                    bk2.publish() if tt == 15 else None)
            bk2.cwait(VQ)
            emit(VQ, lambda e, b=b, bk2=bk2: e.tensor_copy(
                n2tok[:, b * 32:(b + 1) * 32], bk2.ap[:, 0:32]), inc(s_dve))
            bk_gate_now(bk2, s_dve)
            N2Q = (s_dve.h, s_dve.n)
            for hh in range(2):
                h = 2 * b + hh
                bk3 = banks[3]
                bk_acquire(bk3)
                TQ(lambda e, g=N2Q: e.wait_ge(g[0], g[1]))
                emit(TQ, lambda e, b=b, hh=hh, bk3=bk3: e.transpose(
                    bk3.ap[0:16, 0:128],
                    n2tok[:, b * 32 + hh:b * 32 + 32:2],
                    eyef[:, :]), bk3.publish())
                bk3.cwait(VQ)
                emit(VQ, lambda e, bk3=bk3: e.tensor_copy(
                    trow[0:16, 0:128], bk3.ap[0:16, 0:128]), inc(s_dve))
                bk_gate_now(bk3, s_dve)
                wait(GQ, s_dve)
                emit(GQ, lambda e: e.dma_start(
                    out=wrow[0:1, 0:T], in_=trow[0:16, 0:128]), inc(s_hd, 16))
                wait(TQ, s_hd)
                for rh in range(2):
                    for ch in range(4):
                        bkq = banks[4 + (ch % 2)]
                        bk_acquire(bkq)
                        emit(TQ, lambda e, hh=hh, h=h, rh=rh, ch=ch, bkq=bkq: e.matmul(
                            bkq.ap[:, :],
                            rfds[hh * 64:(hh + 1) * 64,
                                 h * R + rh * 128:h * R + rh * 128 + 128],
                            qfeat[hh * 64:(hh + 1) * 64, ch * 512:(ch + 1) * 512],
                            start=True, stop=False))
                        emit(TQ, lambda e, rh=rh, ch=ch, bkq=bkq: e.matmul(
                            bkq.ap[:, :], negones[:, :],
                            wrow[0:1, ch * 512:(ch + 1) * 512],
                            start=False, stop=True), bkq.publish())
                        bkq.cwait(AQ)
                        dstq = [qp0, qp1][rh]
                        emit(AQ, lambda e, dstq=dstq, ch=ch, bkq=bkq: e.activation(
                            dstq[:, ch * 512:(ch + 1) * 512], bkq.ap[:, :], AF.Exp,
                            bias=0.0, scale=1.0), inc(s_act))
                        bk_gate_now(bkq, s_act)
                wait(GQ, s_cc)
                g = getattr(performer, "_kvg", None)
                if g is not None:
                    GQ(lambda e, g=g: e.wait_ge(g[0], g[1]))
                emit(GQ, lambda e, h=h: e.dma_start(
                    out=kvr[:, :], in_=d_kvout.tensor.ap()[:, h * 130:(h + 1) * 130]),
                    inc(s_ld, 16))
                wait(TQ, s_ld)
                wait(TQ, s_act)
                for ch in range(4):
                    bka = banks[6 + (ch % 2)]
                    bk_acquire(bka)
                    emit(TQ, lambda e, ch=ch, bka=bka: e.matmul(
                        bka.ap[0:65, :], kvr[:, 0:65],
                        qp0[:, ch * 512:(ch + 1) * 512], start=True, stop=False))
                    emit(TQ, lambda e, ch=ch, bka=bka: e.matmul(
                        bka.ap[0:65, :], kvr[:, 65:130],
                        qp1[:, ch * 512:(ch + 1) * 512], start=False, stop=True),
                        bka.publish())
                    bka.cwait(VQ)
                    bka.cwait(AQ)
                    emit(AQ, lambda e, bka=bka: e.activation(
                        rcpf, bka.ap[64:65, 0:512], AF.Ln, bias=1e-6,
                        scale=1.0))
                    emit(AQ, lambda e: e.activation(
                        rcpb[64:65, 0:512], rcpf, AF.Exp, bias=0.0,
                        scale=-1.0), inc(s_act))
                    wait(VQ, s_act)
                    emit(VQ, lambda e: e.tensor_copy(
                        rrow[0:1, 0:512], rcpb[64:65, 0:512]), inc(s_dve))
                    rg = (s_dve.h, s_dve.n)
                    emit(VQ, lambda e, bka=bka: e.tensor_copy(
                        khat[0:64, 0:512], bka.ap[0:64, 0:512]), inc(s_dve))
                    bk_gate_now(bka, s_dve)
                    bkb = banks[2]
                    bk_acquire(bkb)
                    TQ(lambda e, g=rg: e.wait_ge(g[0], g[1]))
                    emit(TQ, lambda e, bkb=bkb: e.matmul(
                        bkb.ap[0:64, 0:512], onesr[0:1, 0:64], rrow[0:1, 0:512],
                        start=True, stop=True), bkb.publish())
                    bkb.cwait(VQ)
                    emit(VQ, lambda e, b=b, hh=hh, ch=ch, bkb=bkb: e.tensor_tensor(
                        s1b(b)[hh * 64:(hh + 1) * 64, ch * 512:(ch + 1) * 512],
                        khat[0:64, 0:512], bkb.ap[0:64, 0:512], ALU.mult),
                        inc(s_dve))
                    bk_gate_now(bkb, s_dve)
                    performer._kvg = (bka.rd.h, bka.rd.n)
        PAF_DONE = (s_dve.h, s_dve.n)

        TQ(lambda e, g=PAF_DONE: e.wait_ge(g[0], g[1]))
        for ob in range(NB):
            slot = wring[:, (ob % 2) * 1024:(ob % 2 + 1) * 1024]
            g = getattr(performer, f"_wg{ob % 2}", None)
            if g is not None:
                GQ(lambda e, g=g: e.wait_ge(g[0], g[1]))
            emit(GQ, lambda e, ob=ob, slot=slot: e.dma_start(
                out=slot[:, :].rearrange("p (c f) -> p c f", f=128),
                in_=d_woT[:, ob * 128:(ob + 1) * 128].rearrange(
                    "(c p) f -> p c f", p=128)), inc(s_ld, 16))
            wait(TQ, s_ld)
            lastbk = None
            for tc in range(4):
                bk = banks[tc % 2]
                bk_acquire(bk)
                for cb in range(NB):
                    emit(TQ, lambda e, cb=cb, tc=tc, slot=slot, bk=bk: e.matmul(
                        bk.ap[:, :], slot[:, cb * 128:(cb + 1) * 128],
                        s1b(cb)[:, tc * 512:(tc + 1) * 512],
                        start=(cb == 0), stop=(cb == NB - 1)),
                        bk.publish() if cb == NB - 1 else None)
                bk.cwait(VQ)
                wait(VQ, s_st)
                emit(VQ, lambda e, ob=ob, tc=tc, bk=bk: e.tensor_scalar(
                    stage[:, 512:1024], bk.ap[:, :], bos[:, ob:ob + 1], None,
                    ALU.add), inc(s_dve))
                bk_gate_now(bk, s_dve)
                wait(SQ, s_dve)
                emit(SQ, lambda e, ob=ob, tc=tc: e.dma_start(
                    out=d_ssp[ob * 128:(ob + 1) * 128, tc * 512:(tc + 1) * 512],
                    in_=stage[:, 512:1024]), inc(s_st, 16))
                lastbk = bk
            setattr(performer, f"_wg{ob % 2}", (lastbk.rd.h, lastbk.rd.n))

    def load_layer_consts(l):
        SQ(lambda e, g=ygate[0]: e.wait_ge(g[0], g[1]))
        GQ(lambda e, g=ygate[0]: e.wait_ge(g[0], g[1]))
        emit(SQ, lambda e, l=l: e.dma_start(out=b1s[:, :], in_=d_b1c[l]), inc(s_ld, 16))
        emit(SQ, lambda e, l=l: e.dma_start(out=b2s[:, :], in_=d_b2c[l]), inc(s_ld, 16))
        emit(GQ, lambda e, l=l: e.dma_start(
            out=srow_s[:, :], in_=d_srow[0:1, l * 128:(l + 1) * 128]), inc(s_ld, 16))
        emit(GQ, lambda e, l=l: e.dma_start(
            out=glbt[:, :].rearrange("r (b p) -> r b p", p=128),
            in_=d_glb[l].rearrange("b r p -> r b p")), inc(s_ld, 16))
        emit(GQ, lambda e, l=l: e.dma_start(
            out=indq[:, :].rearrange("h (b p) -> h b p", p=128),
            in_=d_ind[l, 0].rearrange("b h p -> h b p")), inc(s_ld, 16))
        emit(GQ, lambda e, l=l: e.dma_start(
            out=indk[:, :].rearrange("h (b p) -> h b p", p=128),
            in_=d_ind[l, 1].rearrange("b h p -> h b p")), inc(s_ld, 16))

    def compute_cbias():
        bk3 = banks[3]
        bk_acquire(bk3)
        wait(TQ, s_ld)
        emit(TQ, lambda e: e.matmul(bk3.ap[:, 0:128], onesr[:, :],
             srow_s[0:1, 0:128], start=True, stop=True), bk3.publish())
        bk3.cwait(VQ)
        wait(VQ, s_act)
        emit(VQ, lambda e: e.reduce_max(cbias[:, 1:2], bk3.ap[:, 0:64], axis=AX.X,
             apply_absolute_value=True), inc(s_dve))
        emit(VQ, lambda e: e.reduce_max(stage[:, 0:1], bk3.ap[:, 64:128], axis=AX.X,
             apply_absolute_value=True), inc(s_dve))
        bk_gate_now(bk3, s_dve)
        wait(VQ, s_dve)
        emit(VQ, lambda e: e.tensor_tensor(stage[:, 1:2], cbias[:, 1:2],
             stage[:, 0:1], ALU.mult), inc(s_dve))
        wait(VQ, s_dve)
        emit(VQ, lambda e: e.tensor_scalar(cbias[:, 0:1], stage[:, 1:2], -8.0, None,
             ALU.mult), inc(s_dve))
        if "cbdbg" in dump_outs:
            dd = dump_outs["cbdbg"]
            wait(GQ, s_dve)
            emit(GQ, lambda e, dd=dd: e.dma_start(
                out=dd[0:128, 0:2], in_=stage[:, 0:2]), inc(s_hd, 16))
            emit(GQ, lambda e, dd=dd: e.dma_start(
                out=dd[0:128, 2:4], in_=cbias[:, 0:2]), inc(s_hd, 16))
            emit(GQ, lambda e, dd=dd: e.dma_start(
                out=dd[128:129, 0:128], in_=srow_s[:, :]), inc(s_hd, 16))
            wait(GQ, s_hd)
            for Qx in (TQ, VQ, AQ):
                wait(Qx, s_hd)

    def sim_range(kc):
        r0 = max(0, 128 * (kc - 2))
        r1 = min(T, 128 * kc + 128)
        return r0, r1 - r0

    def mask_view(kc, M):
        if kc == 0:
            return masks_s[:, 384:384 + 128]
        if kc == 1:
            return masks_s[:, 512:512 + 256]
        return masks_s[:, 0:M]

    def inner_layer(l, first=False):
        load_layer_consts(l)
        compute_cbias()
        wvres = pool[:, 0:8 * 1024]
        GQ(lambda e, g=ygate[0]: e.wait_ge(g[0], g[1]))
        for cb in range(NB):
            emit(GQ, lambda e, l=l, cb=cb: e.dma_start(
                out=wvres[:, cb * 1024:(cb + 1) * 1024].rearrange(
                    "p (f c) -> p f c", c=128),
                in_=d_wqkv[l, 16:24, cb].rearrange("f p c -> p f c")), inc(s_ld, 16))
        wait(TQ, s_ld)
        TQ(lambda e, g=ygate[0]: e.wait_ge(g[0], g[1]))
        for tt in range(NKT):
            for fc in range(2):
                bk = banks[(tt * 2 + fc) % 2]
                bk_acquire(bk)
                for cb in range(NB):
                    off = (2 if cb < 4 else 1) + 128 * tt
                    emit(TQ, lambda e, cb=cb, off=off, fc=fc, bk=bk: e.matmul(
                        bk.ap[:, :], yb(cb)[:, off:off + 128],
                        wvres[:, cb * 1024 + fc * 512:cb * 1024 + (fc + 1) * 512],
                        start=(cb == 0), stop=(cb == NB - 1)),
                        bk.publish() if cb == NB - 1 else None)
                bk.cwait(VQ)
                emit(VQ, lambda e, tt=tt, fc=fc, bk=bk: e.tensor_copy(
                    vt(tt).rearrange("p (h d) -> p h d", d=65)[:, fc * 8:(fc + 1) * 8, 0:64],
                    bk.ap[:, :].rearrange("p (h d) -> p h d", d=64)), inc(s_dve))
                bk_gate_now(bk, s_dve)
        VDONE = (s_dve.h, s_dve.n)

        for b in range(NB):
            prep_gate = None
            for (tens, wfb, TW, dst_hat, indt, tabA_, tabB_) in (
                    ("q", b, T, qhat, indq, tabAq, tabBq),
                    ("k", 8 + b, KT, khat, indk, tabAk, tabBk)):
                slotpar = 0 if tens == "q" else 1
                slot = wring[:, slotpar * 1024:(slotpar + 1) * 1024]
                g = getattr(inner_layer, f"_wg{slotpar}", None)
                if g is not None:
                    GQ(lambda e, g=g: e.wait_ge(g[0], g[1]))
                emit(GQ, lambda e, l=l, wfb=wfb, slot=slot: e.dma_start(
                    out=slot[:, :].rearrange("p (c f) -> p c f", f=128),
                    in_=d_wqkv[l, wfb].rearrange("c p f -> p c f")),
                    inc(s_ld, 16))
                wait(TQ, s_ld)
                nch = (TW + 511) // 512
                lastbk = None
                for ch in range(nch):
                    off = ch * 512
                    w = min(512, TW - off)
                    bk = banks[ch % 2]
                    bk_acquire(bk)
                    for cb in range(NB):
                        if tens == "q":
                            src_off = HB + off + (0 if cb < 4 else -1)
                        else:
                            src_off = 2 + off + (0 if cb < 4 else -1)
                        emit(TQ, lambda e, cb=cb, src_off=src_off, w=w, slot=slot,
                             bk=bk: e.matmul(
                                 bk.ap[:, 0:w], slot[:, cb * 128:(cb + 1) * 128],
                                 yb(cb)[:, src_off:src_off + w],
                                 start=(cb == 0), stop=(cb == NB - 1)),
                             bk.publish() if cb == NB - 1 else None)
                    bk.cwait(VQ)
                    emit(VQ, lambda e, off=off, w=w, bk=bk: e.tensor_copy(
                        qfeat[:, off:off + w], bk.ap[:, 0:w]), inc(s_dve))
                    bk_gate_now(bk, s_dve)
                    lastbk = bk
                setattr(inner_layer, f"_wg{slotpar}", (lastbk.rd.h, lastbk.rd.n))
                evac_done = (s_dve.h, s_dve.n)
                AQ(lambda e, g=evac_done: e.wait_ge(g[0], g[1]))
                emit(AQ, lambda e, TW=TW: e.activation(
                    qsq[:, 0:TW], qfeat[:, 0:TW], AF.Square, bias=0.0, scale=1.0),
                    inc(s_act))
                wait(TQ, s_act)
                bk3 = banks[3]
                norm_done = None
                for ch in range(nch):
                    off = ch * 512
                    w = min(512, TW - off)
                    bk_acquire(bk3)
                    emit(TQ, lambda e, b=b, off=off, w=w: e.matmul(
                        bk3.ap[0:16, 0:w], ind16s[:, b * 16:(b + 1) * 16],
                        qsq[:, off:off + w], start=True, stop=True), bk3.publish())
                    bk3.cwait(AQ)
                    emit(AQ, lambda e, w=w: e.activation(
                        lnb16[0:16, 0:w], bk3.ap[0:16, 0:w], AF.Ln, bias=1e-17,
                        scale=1.0))
                    emit(AQ, lambda e, w=w: e.activation(
                        rnorm[0:16, 0:w], lnb16[0:16, 0:w], AF.Exp, bias=0.0,
                        scale=-0.5), inc(s_act))
                    bk_gate_now(bk3, s_act)
                    bk2 = banks[2]
                    bk_acquire(bk2)
                    wait(TQ, s_act)
                    emit(TQ, lambda e, b=b, off=off, w=w, indt=indt: e.matmul(
                        bk2.ap[:, 0:w], indt[:, b * 128:(b + 1) * 128],
                        rnorm[0:16, 0:w], start=True, stop=True), bk2.publish())
                    bk2.cwait(VQ)
                    emit(VQ, lambda e, off=off, w=w: e.tensor_tensor(
                        rott0[:, off:off + w], qfeat[:, off:off + w],
                        bk2.ap[:, 0:w], ALU.mult), inc(s_dve))
                    bk_gate_now(bk2, s_dve)
                emit(VQ, lambda e, TW=TW, tabA_=tabA_: e.tensor_tensor(
                    qsq[:, 0:TW], rott0[:, 0:TW], tabA_[:, 0:TW], ALU.mult))
                for (dst0, src0) in ((0, 32), (32, 0), (64, 96), (96, 64)):
                    emit(VQ, lambda e, dst0=dst0, src0=src0, TW=TW: e.tensor_copy(
                        rott1[dst0:dst0 + 32, 0:TW], rott0[src0:src0 + 32, 0:TW]))
                emit(VQ, lambda e, TW=TW, tabB_=tabB_: e.tensor_tensor(
                    rott1[:, 0:TW], rott1[:, 0:TW], tabB_[:, 0:TW], ALU.mult))
                emit(VQ, lambda e, TW=TW, dst_hat=dst_hat: e.tensor_tensor(
                    dst_hat[:, 0:TW], qsq[:, 0:TW], rott1[:, 0:TW], ALU.add),
                    inc(s_dve))
                prep_gate = (s_dve.h, s_dve.n)

            if b == 0 and f"pr{l}" in dump_outs:
                dd = dump_outs[f"pr{l}"]
                GQ(lambda e, g=prep_gate: e.wait_ge(g[0], g[1]))
                emit(GQ, lambda e, dd=dd: e.dma_start(
                    out=dd[0:128, 0:T], in_=qhat[:, :]), inc(s_hd, 16))
                emit(GQ, lambda e, dd=dd: e.dma_start(
                    out=dd[128:256, 0:2048], in_=khat[:, 0:2048]), inc(s_hd, 16))
                emit(GQ, lambda e, dd=dd: e.dma_start(
                    out=dd[256:384, 0:2], in_=cbias[:, :]), inc(s_hd, 16))
                emit(GQ, lambda e, dd=dd: e.dma_start(
                    out=dd[384:512, 0:1040],
                    in_=vt(4)[:, 0:1040]),
                    inc(s_hd, 16))
                wait(GQ, s_hd)
            TQ(lambda e, g=prep_gate: e.wait_ge(g[0], g[1]))
            TQ(lambda e, g=VDONE: e.wait_ge(g[0], g[1]))
            for hh in range(2):
                h = 2 * b + hh
                hb = hh * 64
                eslot_gate = [None] * 6
                eraw_gate = [None] * 2
                for kc in range(NKT):
                    r0, M = sim_range(kc)
                    bk = banks[4 + (kc % 2)]
                    bk_acquire(bk)
                    emit(TQ, lambda e, hb=hb, kc=kc, r0=r0, M=M, bk=bk: e.matmul(
                        bk.ap[0:128, 0:M], khat[hb:hb + 64, kc * 128:(kc + 1) * 128],
                        qhat[hb:hb + 64, r0:r0 + M], start=True, stop=True),
                        bk.publish())
                    er = Eraw[kc % 2]
                    if eraw_gate[kc % 2] is not None:
                        AQ(lambda e, g=eraw_gate[kc % 2]: e.wait_ge(g[0], g[1]))
                    bk.cwait(AQ)
                    emit(AQ, lambda e, er=er, M=M, bk=bk: e.activation(
                        er[:, 0:M], bk.ap[:, 0:M], AF.Exp, bias=cbias[:, 0:1],
                        scale=8.0), inc(s_act))
                    bk_gate_now(bk, s_act)
                    es = Etile[kc % 6]
                    if eslot_gate[kc % 6] is not None:
                        VQ(lambda e, g=eslot_gate[kc % 6]: e.wait_ge(g[0], g[1]))
                    wait(VQ, s_act)
                    emit(VQ, lambda e, es=es, er=er, kc=kc, M=M: e.tensor_tensor(
                        es[:, 0:M], er[:, 0:M], mask_view(kc, M), ALU.mult),
                        inc(s_dve))
                    eraw_gate[kc % 2] = (s_dve.h, s_dve.n)
                    Eready = (s_dve.h, s_dve.n)
                    if b == 0 and hh == 0 and kc == 4 and "eav" in dump_outs:
                        dd = dump_outs["eav"]
                        GQ(lambda e, g=Eready: e.wait_ge(g[0], g[1]))
                        emit(GQ, lambda e, dd=dd: e.dma_start(
                            out=dd[0:128, 0:384], in_=Etile[4][:, :]), inc(s_hd, 16))
                        emit(GQ, lambda e, dd=dd: e.dma_start(
                            out=dd[128:256, 0:384], in_=Eraw[0][:, :]), inc(s_hd, 16))
                        wait(GQ, s_hd)
                        for Qx in (TQ, VQ, AQ):
                            wait(Qx, s_hd)
                    for g_ in range(4):
                        if min(17, 4 * g_ + 5) != kc:
                            continue
                        avb = banks[6 + (g_ % 2)]
                        bk_acquire(avb)
                        TQ(lambda e, g=Eready: e.wait_ge(g[0], g[1]))
                        kcs = [k2 for k2 in range(NKT)
                               if max(0, 128 * (k2 - 2)) < 512 * (g_ + 1)
                               and min(T, 128 * k2 + 128) > 512 * g_]
                        for j, k2 in enumerate(kcs):
                            rr0 = max(0, 128 * (k2 - 2))
                            qlo = max(512 * g_, rr0)
                            qhi = min(512 * (g_ + 1), min(T, 128 * k2 + 128))
                            emit(TQ, lambda e, h=h, k2=k2, j=j, qlo=qlo, qhi=qhi,
                                 rr0=rr0, g_=g_, avb=avb: e.matmul(
                                     avb.ap[0:65, qlo - 512 * g_:qhi - 512 * g_],
                                     vt(k2)[:, h * 65:(h + 1) * 65],
                                     Etile[k2 % 6][:, qlo - rr0:qhi - rr0],
                                     start=(j == 0), stop=(j == len(kcs) - 1)),
                                 avb.publish() if j == len(kcs) - 1 else None)
                        for k2 in kcs:
                            eslot_gate[k2 % 6] = (avb.rd.h, avb.rd.n)
                        avb.cwait(VQ)
                        avb.cwait(AQ)
                        bk2 = banks[2]
                        if bk2.gate is not None:
                            VQ(lambda e, g=bk2.gate: e.wait_ge(g[0], g[1]))
                        emit(AQ, lambda e, avb=avb: e.activation(
                            rcpf, avb.ap[64:65, 0:512], AF.Ln,
                            bias=0.0, scale=1.0))
                        emit(AQ, lambda e: e.activation(
                            rcpb[64:65, 0:512], rcpf, AF.Exp,
                            bias=0.0, scale=-1.0), inc(s_act))
                        wait(VQ, s_act)
                        emit(VQ, lambda e: e.tensor_copy(
                            rrow[0:1, 0:512], rcpb[64:65, 0:512]), inc(s_dve))
                        rg = (s_dve.h, s_dve.n)
                        emit(VQ, lambda e, avb=avb: e.tensor_copy(
                            rott0[0:64, 0:512], avb.ap[0:64, 0:512]), inc(s_dve))
                        bk_gate_now(avb, s_dve)
                        bk_acquire(bk2)
                        TQ(lambda e, g=rg: e.wait_ge(g[0], g[1]))
                        emit(TQ, lambda e, bk2=bk2: e.matmul(
                            bk2.ap[0:64, 0:512], onesr[0:1, 0:64],
                            rrow[0:1, 0:512],
                            start=True, stop=True), bk2.publish())
                        bk2.cwait(VQ)
                        emit(VQ, lambda e, b=b, hb=hb, g_=g_, bk2=bk2: e.tensor_tensor(
                            s1b(b)[hb:hb + 64, g_ * 512:(g_ + 1) * 512],
                            rott0[0:64, 0:512], bk2.ap[0:64, 0:512], ALU.mult),
                            inc(s_dve))
                        bk_gate_now(bk2, s_dve)
                        if b == 0 and hh == 0 and g_ == 0 and "eav" in dump_outs:
                            dd = dump_outs["eav"]
                            wait(GQ, s_dve)
                            wait(GQ, s_act)
                            emit(GQ, lambda e, dd=dd: e.dma_start(
                                out=dd[256:320, 0:512], in_=rott0[0:64, 0:512]),
                                inc(s_hd, 16))
                            emit(GQ, lambda e, dd=dd: e.dma_start(
                                out=dd[320:321, 0:512], in_=rcpb[64:65, 0:512]),
                                inc(s_hd, 16))
                            wait(GQ, s_hd)
                            for Qx in (TQ, VQ, AQ):
                                wait(Qx, s_hd)
        AFDONE = (s_dve.h, s_dve.n)

        TQ(lambda e, g=AFDONE: e.wait_ge(g[0], g[1]))
        for ob in range(NB):
            slot = wring[:, (ob % 2) * 1024:((ob % 2) + 1) * 1024]
            g = getattr(inner_layer, f"_wg{ob % 2}", None)
            if g is not None:
                GQ(lambda e, g=g: e.wait_ge(g[0], g[1]))
            emit(GQ, lambda e, l=l, ob=ob, slot=slot: e.dma_start(
                out=slot[:, :].rearrange("p (c f) -> p c f", f=128),
                in_=d_wo[l, ob].rearrange("c p f -> p c f")),
                inc(s_ld, 16))
            wait(TQ, s_ld)
            lastbk = None
            for tc in range(4):
                bk = banks[tc % 2]
                bk_acquire(bk)
                for cb in range(NB):
                    emit(TQ, lambda e, cb=cb, tc=tc, slot=slot, bk=bk: e.matmul(
                        bk.ap[:, :], slot[:, cb * 128:(cb + 1) * 128],
                        s1b(cb)[:, tc * 512:(tc + 1) * 512],
                        start=(cb == 0), stop=(cb == NB - 1)),
                        bk.publish() if cb == NB - 1 else None)
                bk.cwait(VQ)
                emit(VQ, lambda e, ob=ob, tc=tc, bk=bk: e.tensor_tensor(
                    yb(ob)[:, HB + tc * 512:HB + (tc + 1) * 512],
                    yb(ob)[:, HB + tc * 512:HB + (tc + 1) * 512],
                    bk.ap[:, :], ALU.add), inc(s_dve))
                bk_gate_now(bk, s_dve)
                lastbk = bk
            setattr(inner_layer, f"_wg{ob % 2}", (lastbk.rd.h, lastbk.rd.n))
        YA_DONE = (s_dve.h, s_dve.n)
        if f"ya{l}" in dump_outs or f"af{l}" in dump_outs:
            for nm_, srcfn in ((f"ya{l}", lambda b_: yb(b_)[:, HB:]),
                               (f"af{l}", lambda b_: s1b(b_)[:, :])):
                if nm_ not in dump_outs:
                    continue
                dd = dump_outs[nm_]
                gg = YA_DONE if nm_.startswith("ya") else AFDONE
                GQ(lambda e, g=gg: e.wait_ge(g[0], g[1]))
                for b_ in range(NB):
                    emit(GQ, lambda e, b_=b_, dd=dd, srcfn=srcfn: e.dma_start(
                        out=dd[b_ * 128:(b_ + 1) * 128, :], in_=srcfn(b_)),
                        inc(s_hd, 16))
            wait(GQ, s_hd)
            for Qx in (TQ, VQ, AQ, SQ):
                wait(Qx, s_hd)

        VQ(lambda e, g=YA_DONE: e.wait_ge(g[0], g[1]))
        emit(VQ, lambda e: e.tensor_copy(
            tcol[:, 0:NB],
            y[:, :].rearrange("p (b t) -> p b t", t=TTW)[:, :, TTW - 1:TTW]
            .rearrange("p b t -> p (b t)")), inc(s_dve))
        wait(GQ, s_dve)
        emit(GQ, lambda e: e.dma_start(
            out=d_tbin.tensor.ap(), in_=tcol[:, 0:NB]), inc(s_hd, 16))
        wait(GQ, s_hd)
        emit(GQ, lambda e: e.collective_compute(
            "AllGather", ALU.bypass, replica_groups=[list(range(NCORES))],
            ins=[d_tbin.tensor.ap().opt()], outs=[d_tbout.tensor.ap().opt()]),
            inc(s_cc))
        wait(GQ, s_cc)
        emit(GQ, lambda e: e.indirect_dma_start(
            out=tcol[:, NB:2 * NB], out_offset=None,
            in_=d_tbout.tensor.ap(),
            in_offset=bass.IndirectOffsetOnAxis(ap=hidx_s[:, :], axis=0)),
            inc(s_hd, 16))
        wait(VQ, s_hd)
        emit(VQ, lambda e: e.tensor_scalar(
            y[:, :].rearrange("p (b t) -> p b t", t=TTW)[:, :, HB - 1:HB]
            .rearrange("p b t -> p (b t)"),
            tcol[:, NB:2 * NB], hsc[:, 0:1], None, ALU.mult), inc(s_dve))
        TAIL_DONE = (s_dve.h, s_dve.n)

        TQ(lambda e, g=TAIL_DONE: e.wait_ge(g[0], g[1]))
        AQ(lambda e, g=TAIL_DONE: e.wait_ge(g[0], g[1]))
        w2slot_gate = [None]
        for tc in range(4):
            bsx2, bsq2 = banks[6], banks[7]
            bk_acquire(bsx2)
            bk_acquire(bsq2)
            sq_gate = None
            for cb in range(NB):
                so = HB + tc * 512 + (0 if cb < 4 else -1)
                if sq_gate is not None:
                    AQ(lambda e, g=sq_gate: e.wait_ge(g[0], g[1]))
                emit(AQ, lambda e, cb=cb, so=so: e.activation(
                    khat[:, 0:512], yb(cb)[:, so:so + 512], AF.Square, bias=0.0,
                    scale=1.0), inc(s_act))
                wait(TQ, s_act)
                emit(TQ, lambda e, cb=cb, so=so, bsx2=bsx2: e.matmul(
                    bsx2.ap[0:1, 0:512], ones128[:, 0:1], yb(cb)[:, so:so + 512],
                    start=(cb == 0), stop=(cb == NB - 1)),
                    bsx2.publish() if cb == NB - 1 else None)
                emit(TQ, lambda e, cb=cb, bsq2=bsq2: e.matmul(
                    bsq2.ap[0:1, 0:512], ones128[:, 1:2], khat[:, 0:512],
                    start=(cb == 0), stop=(cb == NB - 1)),
                    bsq2.publish() if cb == NB - 1 else inc(s_pe))
                sq_gate = ((bsq2.rd.h, bsq2.rd.n) if cb == NB - 1
                           else (s_pe.h, s_pe.n))
            bsx2.cwait(VQ)
            bsq2.cwait(VQ)
            emit(VQ, lambda e, bsx2=bsx2: e.tensor_scalar(
                fscr[0:1, 0:512], bsx2.ap[0:1, 0:512], 1.0 / 1024.0, None, ALU.mult),
                inc(s_dve))
            bk_gate_now(bsx2, s_dve)
            emit(VQ, lambda e, bsq2=bsq2: e.tensor_scalar(
                stage[0:1, 0:512], bsq2.ap[0:1, 0:512], 1.0 / 1024.0, None,
                ALU.mult), inc(s_dve))
            bk_gate_now(bsq2, s_dve)
            emit(VQ, lambda e: e.tensor_tensor(
                khat[0:1, 0:512], fscr[0:1, 0:512], fscr[0:1, 0:512], ALU.mult))
            emit(VQ, lambda e: e.tensor_tensor(
                stage[0:1, 0:512], stage[0:1, 0:512], khat[0:1, 0:512],
                ALU.subtract), inc(s_dve))
            wait(AQ, s_dve)
            emit(AQ, lambda e: e.activation(
                stage[0:1, 0:512], stage[0:1, 0:512], AF.Ln, bias=1e-5,
                scale=1.0))
            emit(AQ, lambda e, tc=tc: e.activation(
                rowsA[0:1, tc * 512:(tc + 1) * 512], stage[0:1, 0:512], AF.Exp,
                bias=0.0, scale=-0.5), inc(s_act))
            wait(VQ, s_act)
            emit(VQ, lambda e, tc=tc: e.scalar_tensor_tensor(
                rowsB[0:1, tc * 512:(tc + 1) * 512], fscr[0:1, 0:512], -1.0,
                rowsA[0:1, tc * 512:(tc + 1) * 512], ALU.mult, ALU.mult),
                inc(s_dve))
            ROWS_DONE = (s_dve.h, s_dve.n)
            TQ(lambda e, g=ROWS_DONE: e.wait_ge(g[0], g[1]))
            for cb in range(NB):
                so = HB + tc * 512 + (0 if cb < 4 else -1)
                bk2, bk3 = banks[2], banks[3]
                bk_acquire(bk2)
                bk_acquire(bk3)
                emit(TQ, lambda e, cb=cb, tc=tc, bk2=bk2: e.matmul(
                    bk2.ap[:, :], glbt[0:1, cb * 128:(cb + 1) * 128],
                    rowsA[0:1, tc * 512:(tc + 1) * 512], start=True, stop=True),
                    bk2.publish())
                emit(TQ, lambda e, cb=cb, tc=tc, bk3=bk3: e.matmul(
                    bk3.ap[:, :], glbt[0:2, cb * 128:(cb + 1) * 128],
                    rowsB[0:2, tc * 512:(tc + 1) * 512], start=True, stop=True),
                    bk3.publish())
                bk2.cwait(VQ)
                bk3.cwait(VQ)
                emit(VQ, lambda e, cb=cb, so=so, bk2=bk2: e.tensor_tensor(
                    khat[:, 512:1024], yb(cb)[:, so:so + 512], bk2.ap[:, :],
                    ALU.mult), inc(s_dve))
                bk_gate_now(bk2, s_dve)
                emit(VQ, lambda e, cb=cb, tc=tc, bk3=bk3: e.tensor_tensor(
                    s1b(cb)[:, tc * 512:(tc + 1) * 512], khat[:, 512:1024],
                    bk3.ap[:, :], ALU.add), inc(s_dve))
                bk_gate_now(bk3, s_dve)
            XN_DONE = (s_dve.h, s_dve.n)
            TQ(lambda e, g=XN_DONE: e.wait_ge(g[0], g[1]))
            for fb in range(NFB1):
                slot = w1ring[:, (fb % 2) * 2048:((fb % 2) + 1) * 2048]
                g = getattr(inner_layer, f"_w1g{fb % 2}", None)
                if g is not None:
                    GQ(lambda e, g=g: e.wait_ge(g[0], g[1]))
                emit(GQ, lambda e, l=l, fb=fb, slot=slot: e.dma_start(
                    out=slot[:, :].rearrange("p (c f) -> p c f", f=256),
                    in_=d_w1[l, fb].rearrange("c p f -> p c f")),
                    inc(s_ld, 16))
                wait(TQ, s_ld)
                bka, bkg = banks[0], banks[1]
                bk_acquire(bka)
                bk_acquire(bkg)
                for cb in range(NB):
                    emit(TQ, lambda e, cb=cb, tc=tc, slot=slot, bka=bka: e.matmul(
                        bka.ap[:, :], slot[:, cb * 256:cb * 256 + 128],
                        s1b(cb)[:, tc * 512:(tc + 1) * 512],
                        start=(cb == 0), stop=(cb == NB - 1)),
                        bka.publish() if cb == NB - 1 else None)
                for cb in range(NB):
                    emit(TQ, lambda e, cb=cb, tc=tc, slot=slot, bkg=bkg: e.matmul(
                        bkg.ap[:, :], slot[:, cb * 256 + 128:(cb + 1) * 256],
                        s1b(cb)[:, tc * 512:(tc + 1) * 512],
                        start=(cb == 0), stop=(cb == NB - 1)),
                        bkg.publish() if cb == NB - 1 else None)
                setattr(inner_layer, f"_w1g{fb % 2}", (bkg.rd.h, bkg.rd.n))
                bkg.cwait(AQ)
                emit(AQ, lambda e, fb=fb, bkg=bkg: e.activation(
                    khat[:, 1024:1536], bkg.ap[:, :], AF.Gelu,
                    bias=b1s[:, NFB1 + fb:NFB1 + fb + 1], scale=1.0), inc(s_act))
                bk_gate_now(bkg, s_act)
                bka.cwait(VQ)
                wait(VQ, s_act)
                emit(VQ, lambda e, fb=fb, bka=bka: e.scalar_tensor_tensor(
                    pool[:, fb * 512:(fb + 1) * 512], bka.ap[:, :],
                    b1s[:, fb:fb + 1], khat[:, 1024:1536], ALU.add, ALU.mult),
                    inc(s_dve))
                bk_gate_now(bka, s_dve)
            U_DONE = (s_dve.h, s_dve.n)
            TQ(lambda e, g=U_DONE: e.wait_ge(g[0], g[1]))
            for ob in range(NB):
                if w2slot_gate[0] is not None:
                    GQ(lambda e, g=w2slot_gate[0]: e.wait_ge(g[0], g[1]))
                emit(GQ, lambda e, l=l, ob=ob: e.dma_start(
                    out=w2ring[:, :].rearrange("p (c f) -> p c f", f=128),
                    in_=d_w2[l, ob].rearrange("c p f -> p c f")),
                    inc(s_ld, 16))
                wait(TQ, s_ld)
                bk = banks[4 + (ob % 2)]
                bk_acquire(bk)
                for cb in range(NFB1):
                    emit(TQ, lambda e, cb=cb, bk=bk: e.matmul(
                        bk.ap[:, :], w2ring[:, cb * 128:(cb + 1) * 128],
                        pool[:, cb * 512:(cb + 1) * 512],
                        start=(cb == 0), stop=(cb == NFB1 - 1)),
                        bk.publish() if cb == NFB1 - 1 else None)
                w2slot_gate[0] = (bk.rd.h, bk.rd.n)
                bk.cwait(VQ)
                emit(VQ, lambda e, ob=ob, tc=tc, bk=bk: e.scalar_tensor_tensor(
                    yb(ob)[:, HB + tc * 512:HB + (tc + 1) * 512], bk.ap[:, :],
                    b2s[:, ob:ob + 1],
                    yb(ob)[:, HB + tc * 512:HB + (tc + 1) * 512], ALU.add, ALU.add),
                    inc(s_dve))
                bk_gate_now(bk, s_dve)
        ygate[0] = (s_dve.h, s_dve.n)

        if l < L - 1:
            hstg = scr1[:, 0:NB * HB]
            VQ(lambda e, g=ygate[0]: e.wait_ge(g[0], g[1]))
            emit(VQ, lambda e, hstg=hstg: e.tensor_copy(
                hstg[:, :].rearrange("p (b t) -> p b t", t=HB),
                y[:, :].rearrange("p (b t) -> p b t", t=TTW)[:, :, T:TTW]),
                inc(s_dve))
            wait(GQ, s_dve)
            emit(GQ, lambda e, hstg=hstg: e.dma_start(
                out=d_hbin.tensor.ap(), in_=hstg[:, :]), inc(s_hd, 16))
            wait(GQ, s_hd)
            emit(GQ, lambda e: e.collective_compute(
                "AllGather", ALU.bypass, replica_groups=[list(range(NCORES))],
                ins=[d_hbin.tensor.ap().opt()], outs=[d_hbout.tensor.ap().opt()]),
                inc(s_cc))
            wait(GQ, s_cc)
            emit(GQ, lambda e, hstg=hstg: e.indirect_dma_start(
                out=hstg[:, :], out_offset=None,
                in_=d_hbout.tensor.ap(),
                in_offset=bass.IndirectOffsetOnAxis(ap=hidx_s[:, :], axis=0)),
                inc(s_hd, 16))
            wait(VQ, s_hd)
            emit(VQ, lambda e, hstg=hstg: e.tensor_copy(
                y[:, :].rearrange("p (b t) -> p b t", t=TTW)[:, :, 0:HB],
                hstg[:, :].rearrange("p (b t) -> p b t", t=HB)), inc(s_dve))
            emit(VQ, lambda e: e.tensor_scalar(
                y[:, :].rearrange("p (b t) -> p b t", t=TTW)[:, :, HB - 1:HB],
                y[:, :].rearrange("p (b t) -> p b t", t=TTW)[:, :, HB - 1:HB],
                hsc[:, 0:1], None, ALU.mult), inc(s_dve))
            ygate[0] = (s_dve.h, s_dve.n)

    def dump_stream_y(nm):
        if nm not in dump_outs:
            return
        dd = dump_outs[nm]
        GQ(lambda e, g=ygate[0]: e.wait_ge(g[0], g[1]))
        for b in range(NB):
            emit(GQ, lambda e, b=b, dd=dd: e.dma_start(
                out=dd[b * 128:(b + 1) * 128, :], in_=yb(b)[:, HB:]), inc(s_hd, 16))
        wait(GQ, s_hd)
        for Qx in (TQ, VQ, AQ, SQ):
            wait(Qx, s_hd)

    dump_stream_y("xn")

    if do_perf:
        performer()
        if "s" in dump_outs:
            dd = dump_outs["s"]
            wait(GQ, s_st)
            for b in range(NB):
                emit(GQ, lambda e, b=b, dd=dd: e.dma_start(
                    out=dd[b * 128:(b + 1) * 128, :],
                    in_=d_ssp[b * 128:(b + 1) * 128, :]), inc(s_hd, 16))
            wait(GQ, s_hd)
            for Qx in (TQ, VQ, AQ, SQ):
                wait(Qx, s_hd)
        for i in (0, 1):
            g = getattr(performer, f"_wg{i}", None)
            if g is not None:
                setattr(inner_layer, f"_wg{i}", g)

    for l in range(L):
        inner_layer(l, first=(l == 0))
        ydump = f"y{l}"
        dump_stream_y(ydump)

    AQ(lambda e, g=ygate[0]: e.wait_ge(g[0], g[1]))
    gslot = pool[:, 0:T]
    for b in range(NB):
        emit(AQ, lambda e, b=b: e.activation(
            gslot[:, :], yb(b)[:, HB:], AF.Silu, bias=0.0, scale=1.0), inc(s_act))
        GDONE = (s_act.h, s_act.n)
        for tc in range(4):
            wait(SQ, s_dve)
            emit(SQ, lambda e, b=b, tc=tc: e.dma_start(
                out=stage[:, 0:512],
                in_=d_xT[b * 128:(b + 1) * 128, HB + tc * 512:HB + (tc + 1) * 512]),
                inc(s_ld, 16))
            emit(SQ, lambda e, b=b, tc=tc: e.dma_start(
                out=stage[:, 512:1024],
                in_=d_ssp[b * 128:(b + 1) * 128, tc * 512:(tc + 1) * 512]),
                inc(s_ld, 16))
            wait(VQ, s_ld)
            VQ(lambda e, g=GDONE: e.wait_ge(g[0], g[1]))
            emit(VQ, lambda e, b=b, tc=tc: e.tensor_tensor(
                stage[:, 512:1024], stage[:, 512:1024],
                gslot[:, tc * 512:(tc + 1) * 512], ALU.mult))
            emit(VQ, lambda e: e.tensor_tensor(
                stage[:, 0:512], stage[:, 0:512], stage[:, 512:1024], ALU.add),
                inc(s_dve))
            wait(SQ, s_dve)
            emit(SQ, lambda e, b=b, tc=tc: e.dma_start(
                out=d_out[b * 128:(b + 1) * 128, tc * 512:(tc + 1) * 512],
                in_=stage[:, 0:512]), inc(s_st, 16))
        AQ(lambda e, h=s_st.h, v=s_st.n: e.wait_ge(h, v))
    wait(SQ, s_st, s_st.n)
    with nc.Block() as block:
        @block.tensor
        def _(t):
            for f in OPS["tensor"]:
                f(t)

        @block.scalar
        def _(s):
            for f in OPS["scalar"]:
                f(s)

        @block.vector
        def _(v):
            for f in OPS["vector"]:
                f(v)

        @block.gpsimd
        def _(g):
            for f in OPS["gpsimd"]:
                f(g)

        @block.sync
        def _(s):
            for f in OPS["sync"]:
                f(s)

    return nc, dump_outs


def kernel(**inputs):
    maps = host_prep(inputs)
    nc, _ = build()
    res = run_bass_kernel_spmd(nc, maps, core_ids=list(range(NCORES)))
    parts = [np.asarray(r["out"], np.float32).T for r in res.results]
    return np.concatenate(parts, axis=0).reshape(1, NCORES * T, D)
